# version 08 (final)
# baseline (speedup 1.0000x reference)
import contextlib
import numpy as np
import ml_dtypes
import concourse.bass as bass
import concourse.mybir as mybir
from concourse.bass_utils import run_bass_kernel_spmd
from concourse.bass_core_rust import IndirectOffsetOnAxis

F32 = mybir.dt.float32
BF16 = mybir.dt.bfloat16
U32 = mybir.dt.uint32
AF = mybir.ActivationFunctionType
ALU = mybir.AluOpType
AX = mybir.AxisListType

D = 2048
N = 4096
H = 16
HD = 128
CH = 128
NCH = N // CH
OWN = 2048
NOC = OWN // CH
DC = D // 128
SCALE = HD ** (-0.5)
EPS_L2 = 1e-6
EPS_LN = 1e-5
NE = 16
I_FF = 1408
IC = 11
SI_FF = 2816
SIC = 22
TOK_TILE = 512
NQT = OWN // TOK_TILE
POWS = [2, 4, 8, 16, 32, 64]



def _own_rows(hg):
    return np.concatenate([np.arange(512) + 512 * (2 * j + hg) for j in range(4)])


def _rope_tables():
    inv = 1.0 / (10000.0 ** (np.arange(0, HD, 2, dtype=np.float32) / HD))
    th = np.arange(N, dtype=np.float32)[:, None] * inv[None, :]
    return np.cos(th).astype(np.float32), np.sin(th).astype(np.float32)


def _masks(hg):
    i = np.arange(CH)
    trilm = (i[:, None] > i[None, :]).astype(ml_dtypes.bfloat16)
    trium = (i[:, None] < i[None, :]).astype(ml_dtypes.bfloat16)
    tk = np.arange(128)[:, None]
    tq = np.arange(TOK_TILE)[None, :]
    slots = []
    for m in range(8):
        if hg == 0:
            msk = (128 * m + tk <= tq) if m < 4 else np.zeros((128, TOK_TILE), bool)
        else:
            msk = np.ones((128, TOK_TILE), bool) if m < 4 else (128 * (m - 4) + tk <= tq)
        slots.append(msk.astype(ml_dtypes.bfloat16))
    return trilm, trium, np.concatenate(slots, axis=1)


I_SEGS_H = [(0, 4), (4, 4), (8, 3)]
D_SEGS_H = [(0, 4), (4, 4), (8, 4), (12, 4)]
OFF1 = [0, 8192, 16384]
W13_COLS = 22528
OFF2 = [0, 5632, 11264, 16896]


def _pack13(w):
    blocks = []
    for ic0, icn in I_SEGS_H:
        b = w[ic0 * 128:(ic0 + icn) * 128, :]
        b = b.T.reshape(16, 128, icn * 128).transpose(1, 0, 2).reshape(128, -1)
        blocks.append(b)
    return np.concatenate(blocks, axis=1)


def _pack2(w):
    blocks = []
    for dc0, dcn in D_SEGS_H:
        b = w[dc0 * 128:(dc0 + dcn) * 128, :]
        b = b.T.reshape(11, 128, dcn * 128).transpose(1, 0, 2).reshape(128, -1)
        blocks.append(b)
    return np.concatenate(blocks, axis=1)


def stage_inputs(core, L0_attn_x, L0_attn_Wq, L0_attn_Wk, L0_attn_Wv, L0_attn_Wo,
                 L0_attn_Wb, L0_ffn_gate_w, L0_ffn_w1, L0_ffn_w2, L0_ffn_w3,
                 L0_ffn_sw1, L0_ffn_sw2, L0_ffn_sw3):
    b, hg = core // 2, core % 2
    x = np.asarray(L0_attn_x)
    rows = _own_rows(hg)
    cos_t, sin_t = _rope_tables()
    trilm, trium, cmask = _masks(hg)
    bf = ml_dtypes.bfloat16
    w1 = np.asarray(L0_ffn_w1)
    w2 = np.asarray(L0_ffn_w2)
    w3 = np.asarray(L0_ffn_w3)
    sw1 = np.asarray(L0_ffn_sw1)
    sw2 = np.asarray(L0_ffn_sw2)
    sw3 = np.asarray(L0_ffn_sw3)
    sw1p = np.concatenate([_pack13(sw1[k * 1408:(k + 1) * 1408, :]) for k in range(2)], axis=1)
    sw3p = np.concatenate([_pack13(sw3[k * 1408:(k + 1) * 1408, :]) for k in range(2)], axis=1)
    sw2p = np.concatenate([_pack2(sw2[:, k * 1408:(k + 1) * 1408]) for k in range(2)], axis=1)
    w1p = np.stack([_pack13(w1[e]) for e in range(NE)])
    w3p = np.stack([_pack13(w3[e]) for e in range(NE)])
    w2p = np.stack([_pack2(w2[e]) for e in range(NE)])
    return {
        "xT": np.ascontiguousarray(x[b].T).astype(np.float32, copy=False),
        "xq": np.ascontiguousarray(x[b][rows].T).astype(np.float32, copy=False),
        "wq": np.ascontiguousarray(L0_attn_Wq),
        "wk": np.ascontiguousarray(L0_attn_Wk),
        "wv": np.ascontiguousarray(L0_attn_Wv),
        "wb": np.ascontiguousarray(L0_attn_Wb),
        "wo": np.ascontiguousarray(L0_attn_Wo),
        "gwT": np.ascontiguousarray(np.asarray(L0_ffn_gate_w).T),
        "w1p": np.ascontiguousarray(w1p),
        "w2p": np.ascontiguousarray(w2p),
        "w3p": np.ascontiguousarray(w3p),
        "sw1p": np.ascontiguousarray(sw1p),
        "sw2p": np.ascontiguousarray(sw2p),
        "sw3p": np.ascontiguousarray(sw3p),
        "cos_t": cos_t, "sin_t": sin_t,
        "qcos": np.ascontiguousarray(cos_t[rows]),
        "qsin": np.ascontiguousarray(sin_t[rows]),
        "trilm": trilm, "trium": trium, "cmask": cmask,
        "ident_bf": np.eye(128, dtype=bf),
        "ident_f32": np.eye(128, dtype=np.float32),
        "eps_l2": np.full((128, 1), EPS_L2, np.float32),
        "eps_ln": np.full((1, 1), EPS_LN, np.float32),
        "iota16_in": np.broadcast_to(np.arange(NE, dtype=np.float32), (128, NE)).copy(),
        "iotatok_in": (np.arange(16, dtype=np.uint32)[None, :] * 128
                       + np.arange(128, dtype=np.uint32)[:, None]).copy(),
        "epslncol": np.full((128, 1), EPS_LN, np.float32),
    }



class Prog:
    ENGS = ("sync", "gpsimd", "tensor", "vector", "scalar")

    def __init__(self, nc, stack):
        self.nc = nc
        self.stack = stack
        self.q = {e: [] for e in self.ENGS}
        self.sems = {}
        self.cnt = {}
        self.lastwait = {e: {} for e in self.ENGS}

    def sem(self, name):
        h = self.stack.enter_context(self.nc.semaphore(name))
        self.sems[name] = h
        self.cnt[name] = 0

    def inc(self, name, n=1):
        self.cnt[name] += n
        return self.cnt[name]

    def E(self, eng, fn):
        self.q[eng].append(fn)

    def W(self, eng, sem, val):
        if val <= 0:
            return
        lw = self.lastwait[eng]
        if lw.get(sem, -1) >= val:
            return
        lw[sem] = val
        h = self.sems[sem]
        self.q[eng].append(lambda e, h=h, v=val: e.wait_ge(h, v))

    def emit(self, block):
        dec = {"sync": block.sync, "gpsimd": block.gpsimd, "tensor": block.tensor,
               "vector": block.vector, "scalar": block.scalar}
        for eng in self.ENGS:
            ops = self.q[eng]
            for i in range(0, len(ops), 4000):
                sub = ops[i:i + 4000]
                dec[eng](lambda e, sub=sub: [f(e) for f in sub])


def build_program(nch=NCH, noc=NOC, nqt=NQT, nex=NE, dbg_phase_a=False,
                  dbg_zero_ot=False, phases=("A", "B", "C")):
    nc = bass.Bass("TRN2", target_bir_lowering=False, debug=False)
    stack = contextlib.ExitStack()
    stack.__enter__()

    def dram(name, shape, dtype, kind="ExternalInput"):
        return nc.dram_tensor(name, list(shape), dtype, kind=kind)

    xT = dram("xT", (D, N), F32).ap()
    xq = dram("xq", (D, OWN), F32).ap()
    wq = dram("wq", (D, D), F32).ap()
    wk = dram("wk", (D, D), F32).ap()
    wv = dram("wv", (D, D), F32).ap()
    wb = dram("wb", (D, H), F32).ap()
    wo = dram("wo", (D, D), F32).ap()
    gwT = dram("gwT", (D, NE), F32).ap()
    w1p = dram("w1p", (NE, 128, W13_COLS), BF16).ap()
    w2p = dram("w2p", (NE, 128, W13_COLS), BF16).ap()
    w3p = dram("w3p", (NE, 128, W13_COLS), BF16).ap()
    sw1p = dram("sw1p", (128, 2 * W13_COLS), BF16).ap()
    sw2p = dram("sw2p", (128, 2 * W13_COLS), BF16).ap()
    sw3p = dram("sw3p", (128, 2 * W13_COLS), BF16).ap()
    cos_t = dram("cos_t", (N, 64), F32).ap()
    sin_t = dram("sin_t", (N, 64), F32).ap()
    qcos = dram("qcos", (OWN, 64), F32).ap()
    qsin = dram("qsin", (OWN, 64), F32).ap()
    trilm = dram("trilm", (CH, CH), BF16).ap()
    trium = dram("trium", (CH, CH), BF16).ap()
    cmask = dram("cmask", (128, 8 * TOK_TILE), BF16).ap()
    ident_bf = dram("ident_bf", (128, 128), BF16).ap()
    ident_f32 = dram("ident_f32", (128, 128), F32).ap()
    eps_l2 = dram("eps_l2", (128, 1), F32).ap()
    eps_ln = dram("eps_ln", (1, 1), F32).ap()
    iota16_in = dram("iota16_in", (128, NE), F32).ap()
    iotatok_in = dram("iotatok_in", (128, NE), U32).ap()
    epslncol_in = dram("epslncol", (128, 1), F32).ap()

    outR = dram("outR", (OWN, D), F32, kind="ExternalOutput").ap()

    spill_kind = "ExternalOutput" if dbg_phase_a else "Internal"
    vnew_s = dram("vnew_s", (H, 128, NCH, HD), BF16, kind=spill_kind).ap()
    kbf_s = dram("kbf_s", (H, 128, NCH, CH), BF16, kind=spill_kind).ap()
    qf_s = dram("qf_s", (H, 128, NOC, CH), BF16, kind=spill_kind).ap()
    mneg_dbg = dram("mneg_dbg", (1, H), F32, kind=spill_kind).ap() if dbg_phase_a else None
    bhat_dbg = dram("bhat_dbg", (2, NCH, CH, 8), F32, kind=spill_kind).ap() if dbg_phase_a else None
    braw_dbg = dram("braw_dbg", (2, NCH, CH, 8), F32, kind=spill_kind).ap() if dbg_phase_a else None
    krt_dbg = dram("krt_dbg", (CH, 1024), F32, kind=spill_kind).ap() if dbg_phase_a else None
    h1_s = dram("h1_s", (DC, 128, OWN), F32, kind="Internal").ap()
    h1tok_s = dram("h1tok_s", (OWN, D), BF16, kind="Internal").ap()
    h1tokf_s = dram("h1tokf_s", (OWN, D), F32, kind="Internal").ap()
    y_tok_s = dram("y_tok_s", (OWN, D), F32, kind="Internal").ap()
    CAP = 512
    idx_s = dram("idx_s", (NE * CAP, 1), U32, kind="Internal").ap()
    wg_s = dram("wg_s", (NE * CAP, 1), F32, kind="Internal").ap()

    p = Prog(nc, stack)
    for s in ("lsy", "wld", "xc0", "xc1", "cs0", "cs1", "spv", "spk", "spq",
              "ts", "vs", "ss", "gp",
              "hb0", "hb1", "cw", "cwg", "xr0", "xr1", "h1b", "h1f0", "h1f1",
              "ws0", "ws1", "out", "ts2", "vs2", "ss2", "sph",
              "scx", "h1t", "h1tb", "ytw", "gld", "gat", "sca", "pre"):
        p.sem(s)
    L = p.sems

    def sbuf(name, shape, dtype):
        return stack.enter_context(nc.sbuf_tensor(name, shape, dtype))

    def psum(name, shape, dtype=F32):
        return stack.enter_context(nc.psum_tensor(name, shape, dtype))

    ident_b = sbuf("ident_b", [128, 128], BF16)
    ident_f = sbuf("ident_f", [128, 128], F32)
    trilm_b = sbuf("trilm_b", [CH, CH], BF16)
    trium_b = sbuf("trium_b", [CH, CH], BF16)
    cmask_b = sbuf("cmask_b", [128, 8 * TOK_TILE], BF16)
    epsl2_b = sbuf("epsl2_b", [128, 1], F32)
    epsln_b = sbuf("epsln_b", [1, 1], F32)
    ones_col = sbuf("ones_col", [128, 1], BF16)
    ones_row = sbuf("ones_row", [1, 128], BF16)
    ones_col_f = sbuf("ones_col_f", [128, 1], F32)
    ones_row_f = sbuf("ones_row_f", [1, 128], F32)

    def sy_dma(out_ap, in_ap, sem="lsy"):
        p.inc(sem, 16)
        p.E("sync", lambda e, o=out_ap, i=in_ap, s=sem: e.dma_start(out=o, in_=i).then_inc(L[s], 16))
        return p.cnt[sem]

    def gp_dma(out_ap, in_ap, sem="wld"):
        p.inc(sem, 16)
        p.E("gpsimd", lambda e, o=out_ap, i=in_ap, s=sem: e.dma_start(out=o, in_=i).then_inc(L[s], 16))
        return p.cnt[sem]

    for dst, src in ((ident_b[:, :], ident_bf), (ident_f[:, :], ident_f32),
                     (trilm_b[:, :], trilm), (trium_b[:, :], trium),
                     (cmask_b[:, :], cmask), (epsl2_b[:, :], eps_l2),
                     (epsln_b[:, :], eps_ln)):
        sy_dma(dst, src)
    const_ld = p.cnt["lsy"]

    cur = {"t": "ts", "v": "vs", "s": "ss"}

    def t_wait(sem, val):
        p.W("tensor", sem, val)

    def v_wait(sem, val):
        p.W("vector", sem, val)

    def s_wait(sem, val):
        p.W("scalar", sem, val)

    def mm(psum_ap, lhsT_ap, rhs_ap, start, stop, inc=False):
        sn = cur["t"]
        if inc:
            p.inc(sn)
        p.E("tensor", lambda e, o=psum_ap, a=lhsT_ap, b=rhs_ap, st=start, sp=stop, i=inc, sn=sn:
            (e.matmul(o, a, b, start=st, stop=sp).then_inc(L[sn], 1) if i
             else e.matmul(o, a, b, start=st, stop=sp)))
        return p.cnt[sn]

    def transpose(psum_ap, in_ap, ident=None):
        sn = cur["t"]
        p.inc(sn)
        idn = ident if ident is not None else ident_b
        p.E("tensor", lambda e, o=psum_ap, a=in_ap, idn=idn, sn=sn:
            e.transpose(o, a, idn[:, :]).then_inc(L[sn], 1))
        return p.cnt[sn]

    def v_op(fn, inc=True):
        sn = cur["v"]
        p.W("vector", sn, p.cnt[sn])
        p.inc(sn)
        p.E("vector", lambda e, f=fn, sn=sn: f(e).then_inc(L[sn], 1))
        return p.cnt[sn]

    def s_op(fn, inc=True):
        sn = cur["s"]
        p.W("scalar", sn, p.cnt[sn])
        p.inc(sn)
        p.E("scalar", lambda e, f=fn, sn=sn: f(e).then_inc(L[sn], 1))
        return p.cnt[sn]

    v_op(lambda e: e.memset(ones_col[:, :], 1.0), inc=False)
    v_op(lambda e: e.memset(ones_row[:, :], 1.0), inc=False)
    v_op(lambda e: e.memset(ones_col_f[:, :], 1.0), inc=False)
    ones_done = v_op(lambda e: e.memset(ones_row_f[:, :], 1.0))

    dwT = sbuf("dwT", [NE, OWN], F32)
    wstack = contextlib.ExitStack()
    wstack.__enter__()
    wres = wstack.enter_context(nc.sbuf_tensor("wres", [128, DC * 2048], BF16, side="right"))
    Mneg = sbuf("Mneg", [128, H], F32)
    braw = sbuf("braw", [CH, 8], F32)
    bhat = sbuf("bhat", [CH, 8], F32)
    krsum = sbuf("krsum", [CH, 8], F32)
    krstd = sbuf("krstd", [CH, 8], F32)
    qrsum = sbuf("qrsum", [CH, H], F32)
    qn_c = sbuf("qn_c", [1, H], F32)
    Mrun = sbuf("Mrun", [1, H], F32)
    Mrow = sbuf("Mrow", [1, H], F32)

    trk = {
        "pp_evac": [0, 0], "ab_evac": 0, "r_evac": [0, 0], "tp_evac": 0, "bc_evac": 0,
        "spill_vn": 0, "spill_kbf": 0, "spill_qf": 0,
    }

    def barrier():
        snap = dict(p.cnt)
        for eng in Prog.ENGS:
            for s_, v_ in snap.items():
                p.W(eng, s_, v_)

    astack = contextlib.ExitStack()
    astack.__enter__()

    def asbuf(name, shape, dtype):
        return astack.enter_context(nc.sbuf_tensor(name, shape, dtype))

    def apsum(name, shape, dtype=F32):
        return astack.enter_context(nc.psum_tensor(name, shape, dtype))

    pp = [apsum(f"pp{i}", [128, 512], F32) for i in range(2)]
    ps_ab = apsum("ps_ab", [128, 512], F32)
    ps_r = [apsum(f"ps_r{i}", [128, 512], F32) for i in range(2)]
    ps_t2q = apsum("ps_t2q", [128, 512], BF16)
    ps_bc = apsum("ps_bc", [128, 512], F32)
    xc_bf = [asbuf(f"xc{i}", [128, DC * CH], BF16) for i in range(2)]
    cos_c = [asbuf(f"cosc{i}", [CH, 64], F32) for i in range(2)]
    sin_c = [asbuf(f"sinc{i}", [CH, 64], F32) for i in range(2)]
    wb_res = asbuf("wb_res", [128, DC * 8], BF16)
    ktk = asbuf("ktk", [CH, 2048], F32)
    krx = asbuf("krx", [CH, 2048], F32)
    scr = asbuf("scr", [CH, 2048], F32)
    t_bf = asbuf("t_bf", [CH, 2048], BF16)
    fbf = asbuf("fbf", [HD, 2048], BF16)
    v_bf = asbuf("v_bf", [CH, 1024], BF16)
    A_bf = asbuf("A_bf", [CH, CH], BF16)
    AT_bf = asbuf("AT_bf", [CH, CH], BF16)
    nAT_bf = asbuf("nAT_bf", [CH, CH], BF16)
    powT = asbuf("powT", [CH, 6 * CH], BF16)
    praw = [asbuf(f"praw{i}", [CH, CH], BF16) for i in range(2)]
    Rbuf = [asbuf(f"Rbuf{i}", [CH, 256], BF16) for i in range(2)]
    U_all = asbuf("U_all", [CH, 8 * HD], BF16)
    Wf_all = asbuf("Wf_all", [HD, 8 * CH], BF16)
    vn_buf = asbuf("vn_buf", [CH, 8 * HD], BF16)
    S_f32 = asbuf("S_f32", [HD, 8 * HD], F32)
    S_bf = asbuf("S_bf", [HD, 8 * HD], BF16)

    def rope_emit(src_view, dst_view, cosb, sinb, tA, tB, after_vs):
        av = tA
        bv = tB
        v_wait("vs", after_vs)
        v_op(lambda e: e.tensor_tensor(av, src_view[:, :, 0:64], cosb, ALU.mult), inc=False)
        d1 = v_op(lambda e: e.tensor_tensor(bv, src_view[:, :, 64:128], sinb, ALU.mult))
        v_wait("vs", d1)
        v_op(lambda e: e.tensor_tensor(dst_view[:, :, 0:64], av, bv, ALU.subtract), inc=False)
        v_op(lambda e: e.tensor_tensor(av, src_view[:, :, 0:64], sinb, ALU.mult), inc=False)
        d2 = v_op(lambda e: e.tensor_tensor(bv, src_view[:, :, 64:128], cosb, ALU.mult))
        v_wait("vs", d2)
        return v_op(lambda e: e.tensor_tensor(dst_view[:, :, 64:128], av, bv, ALU.add))

    def proj_tiles(xbuf, wres_, wwid, tiles, first_waits):
        for ti, (w_off, width, dst) in enumerate(tiles):
            pb = pp[ti % 2]
            for sem, val in first_waits:
                t_wait(sem, val)
            t_wait("vs", trk["pp_evac"][ti % 2])
            for dc in range(DC):
                mm(pb[:, 0:width], xbuf[:, dc * CH:(dc + 1) * CH],
                   wres_[:, dc * wwid + w_off: dc * wwid + w_off + width],
                   start=(dc == 0), stop=(dc == DC - 1), inc=(dc == DC - 1))
            v_wait("ts", p.cnt["ts"])
            trk["pp_evac"][ti % 2] = v_op(
                lambda e, d=dst, pb=pb, w=width: e.tensor_copy(d, pb[:, 0:w]))
        return p.cnt["vs"]

    if "A" in phases:
        t_wait("lsy", const_ld)
        v_wait("lsy", const_ld)
        s_wait("lsy", const_ld)
        ld_wq = gp_dma(wres.ap().rearrange("p (dc n) -> p dc n", n=D),
                       wq.rearrange("(dc p) n -> p dc n", p=128), sem="wld")
        v_op(lambda e: e.memset(Mrun[:, :], 0.0))
        xc_cons = [0, 0]
        rope_cons = [0, 0]

        for c in range(noc):
            bi = c % 2
            p.W("gpsimd", "ts", xc_cons[bi])
            ldx = gp_dma(xc_bf[bi].ap().rearrange("p (dc n) -> p dc n", n=CH),
                         xq.rearrange("(dc p) (nc n) -> p dc nc n", p=128, n=CH)[:, :, c, :],
                         sem=f"xc{bi}")
            p.W("sync", "vs", rope_cons[bi])
            sy_dma(cos_c[bi][:, :], qcos[c * CH:(c + 1) * CH, :], sem=f"cs{bi}")
            lds = sy_dma(sin_c[bi][:, :], qsin[c * CH:(c + 1) * CH, :], sem=f"cs{bi}")
            waits = [(f"xc{bi}", ldx)] + ([("wld", ld_wq)] if c == 0 else [])
            ev = proj_tiles(xc_bf[bi], wres, D,
                            [(t * 512, 512, ktk[:, t * 512:(t + 1) * 512]) for t in range(4)],
                            waits)
            xc_cons[bi] = p.cnt["ts"]
            v_wait(f"cs{bi}", lds)
            qv = ktk.ap().rearrange("p (h d) -> p h d", h=H)
            rv = krx.ap().rearrange("p (h d) -> p h d", h=H)
            cb = cos_c[bi].ap().unsqueeze(1).broadcast_to((CH, H, 64))
            sbr = sin_c[bi].ap().unsqueeze(1).broadcast_to((CH, H, 64))
            tA = scr.ap()[:, 0:1024].rearrange("p (h d) -> p h d", d=64)
            tB = scr.ap()[:, 1024:2048].rearrange("p (h d) -> p h d", d=64)
            rope_done = rope_emit(qv, rv, cb, sbr, tA, tB, ev)
            rope_cons[bi] = rope_done
            cast_done = v_op(lambda e: e.tensor_copy(t_bf[:, :], krx[:, :]))
            s_wait("vs", rope_done)
            sq_done = s_op(lambda e: e.activation(scr[:, :], krx[:, :], AF.Square))
            v_wait("ss", sq_done)
            red = v_op(lambda e: e.tensor_reduce(
                qrsum.ap().unsqueeze(2),
                scr.ap().rearrange("p (h d) -> p h d", h=H), AX.X, ALU.add))
            t_wait("vs", red)
            t_wait("vs", ones_done)
            t_wait("vs", trk["ab_evac"])
            qs_mm = mm(ps_ab[0:1, 0:H], ones_col_f[:, 0:1], qrsum[:, :],
                       start=True, stop=True, inc=True)
            v_wait("ts", qs_mm)
            cp = v_op(lambda e: e.tensor_copy(qn_c[:, :], ps_ab[0:1, 0:H]))
            trk["ab_evac"] = cp
            v_wait("vs", cp)
            v_op(lambda e: e.tensor_tensor(Mrun[:, :], Mrun[:, :], qn_c[:, :], ALU.max))
            v_wait("spq", trk["spill_qf"])
            for g in range(4):
                for q4 in range(4):
                    h = g * 4 + q4
                    t_wait("vs", cast_done)
                    t_wait("vs", trk["tp_evac"])
                    transpose(ps_t2q[:, q4 * 128:(q4 + 1) * 128], t_bf[:, h * HD:(h + 1) * HD])
                tp = p.cnt["ts"]
                v_wait("ts", tp)
                trk["tp_evac"] = v_op(
                    lambda e, g=g: e.tensor_copy(fbf[:, g * 512:(g + 1) * 512], ps_t2q[:, :]))
            p.W("sync", "vs", trk["tp_evac"])
            trk["spill_qf"] = sy_dma(
                qf_s.rearrange("h p nc n -> p h nc n")[:, :, c, :],
                fbf.ap().rearrange("p (h n) -> p h n", h=H), sem="spq")

        s2 = 2.0 * SCALE
        s_wait("vs", p.cnt["vs"])
        mr = s_op(lambda e: e.activation(Mrow[:, :], Mrun[:, :], AF.Sqrt, scale=s2 * s2))
        v_wait("ss", mr)
        mneg_row = v_op(lambda e: e.tensor_scalar(Mrow[:, :], Mrow[:, :], -1.0, None, ALU.mult))
        t_wait("vs", mneg_row)
        t_wait("vs", trk["bc_evac"])
        mb = mm(ps_bc[:, 0:H], ones_row_f[0:1, :], Mrow[0:1, :], start=True, stop=True, inc=True)
        v_wait("ts", mb)
        mneg_done = v_op(lambda e: e.tensor_copy(Mneg[:, :], ps_bc[:, 0:H]))
        trk["bc_evac"] = mneg_done
        if dbg_phase_a:
            p.W("sync", "vs", mneg_done)
            sy_dma(mneg_dbg, Mneg[0:1, :], sem="lsy")

        k_tok = ktk[:, 0:1024]
        kn_v = krx.ap()[:, 0:1024].rearrange("p (h d) -> p h d", h=8)
        kr_v = krx.ap()[:, 1024:2048].rearrange("p (h d) -> p h d", h=8)
        kr_flat = krx[:, 1024:2048]
        ksq = scr[:, 0:1024]
        kb_t = scr[:, 1024:2048]

        v_bf2 = [v_bf, asbuf("v_bfB", [CH, 1024], BF16)]
        t_bf2 = [t_bf, asbuf("t_bfB", [CH, 2048], BF16)]
        fbf2 = [fbf, asbuf("fbfB", [HD, 2048], BF16)]
        Araw_all = asbuf("Araw_all", [CH, 1024], BF16)
        AT_all = asbuf("AT_all", [CH, 1024], BF16)
        nAT_all = asbuf("nAT_all", [CH, 1024], BF16)
        PWb = [asbuf(f"PWb{i}", [CH, 8 * 256], BF16) for i in range(2)]
        R_all = [asbuf(f"R_all{i}", [CH, 8 * 256], BF16) for i in range(2)]
        ps_t2 = ps_t2q

        chain_ps = [ps_ab, ps_bc, ps_r[0], ps_r[1]]

        def raw_slot(h):
            return chain_ps[h // 2][:, (h % 2) * 128:(h % 2) * 128 + 128]

        def T_slot(h):
            return chain_ps[h // 2][:, 256 + (h % 2) * 128: 256 + (h % 2) * 128 + 128]

        def ap_slot(h):
            return chain_ps[h // 2][:, (h % 2) * 256:(h % 2) * 256 + 256]

        trk2 = {
            "chain_free": [("vs", 0)] * 4,
            "pst_free": ("vs", 0),
            "tbf_free": [0, 0],
            "vbf_free": [0, 0],
            "fbf_free": [(0, 0), (0, 0)],
            "ready": [(0, 0), (0, 0)],
        }

        def stage1(c, first_waits):
            bi = c % 2
            p.W("gpsimd", "ts", xc_cons[bi])
            ldx = gp_dma(xc_bf[bi].ap().rearrange("p (dc n) -> p dc n", n=CH),
                         xT.rearrange("(dc p) (nc n) -> p dc nc n", p=128, n=CH)[:, :, c, :],
                         sem=f"xc{bi}")
            p.W("sync", "vs", rope_cons[bi])
            sy_dma(cos_c[bi][:, :], cos_t[c * CH:(c + 1) * CH, :], sem=f"cs{bi}")
            lds = sy_dma(sin_c[bi][:, :], sin_t[c * CH:(c + 1) * CH, :], sem=f"cs{bi}")
            yield
            tiles = [(0, 512, k_tok[:, 0:512]), (512, 512, k_tok[:, 512:1024]),
                     (1024, 512, v_bf2[bi][:, 0:512]), (1536, 512, v_bf2[bi][:, 512:1024])]
            for ti, (w_off, width, dst) in enumerate(tiles):
                pb = pp[ti % 2]
                t_wait(f"xc{bi}", ldx)
                for sem_, val_ in first_waits:
                    t_wait(sem_, val_)
                t_wait("vs", trk["pp_evac"][ti % 2])
                if ti >= 2:
                    t_wait("vs", trk2["vbf_free"][bi])
                for dc in range(DC):
                    mm(pb[:, 0:width], xc_bf[bi][:, dc * CH:(dc + 1) * CH],
                       wres[:, dc * 2048 + w_off: dc * 2048 + w_off + width],
                       start=(dc == 0), stop=(dc == DC - 1), inc=(dc == DC - 1))
                v_wait("ts", p.cnt["ts"])
                trk["pp_evac"][ti % 2] = v_op(
                    lambda e, d=dst, pb=pb, w=width: e.tensor_copy(d, pb[:, 0:w]))
                yield
            pb = pp[0]
            t_wait("vs", trk["pp_evac"][0])
            for dc in range(DC):
                mm(pb[:, 0:8], xc_bf[bi][:, dc * CH:(dc + 1) * CH],
                   wb_res[:, dc * 8: dc * 8 + 8],
                   start=(dc == 0), stop=(dc == DC - 1), inc=(dc == DC - 1))
            v_wait("ts", p.cnt["ts"])
            if dbg_phase_a:
                v_wait("lsy", trk.get("dbg_spill", 0))
            trk["pp_evac"][0] = v_op(lambda e, pb=pb: e.tensor_copy(braw[:, 0:8], pb[:, 0:8]))
            ev = p.cnt["vs"]
            xc_cons[bi] = p.cnt["ts"]
            yield
            s_wait("vs", ev)
            if dbg_phase_a:
                s_wait("lsy", trk.get("dbg_spill", 0))
            bsig = s_op(lambda e: e.activation(bhat[:, :], braw[:, :], AF.Sigmoid))
            v_wait("ss", bsig)
            bdone = v_op(lambda e: e.tensor_scalar(bhat[:, :], bhat[:, :], 2.0, None, ALU.mult))
            s_wait("vs", ev)
            ks_done = s_op(lambda e: e.activation(ksq, k_tok, AF.Square))
            v_wait("ss", ks_done)
            red = v_op(lambda e: e.tensor_reduce(
                krsum.ap()[:, 0:8].unsqueeze(2),
                scr.ap()[:, 0:1024].rearrange("p (h d) -> p h d", h=8), AX.X, ALU.add))
            s_wait("vs", red)
            s_op(lambda e: e.activation(krstd[:, :], krsum[:, :], AF.Ln, bias=epsl2_b[:, :]),
                 inc=False)
            rstd_done = s_op(lambda e: e.activation(krstd[:, :], krstd[:, :], AF.Exp, scale=-0.5))
            v_wait("ss", rstd_done)
            ktv = ktk.ap()[:, 0:1024].rearrange("p (h d) -> p h d", h=8)
            rsv = krstd.ap()[:, 0:8].unsqueeze(2).broadcast_to((CH, 8, HD))
            norm_done = v_op(lambda e: e.tensor_tensor(kn_v, ktv, rsv, ALU.mult))
            v_wait(f"cs{bi}", lds)
            cb = cos_c[bi].ap().unsqueeze(1).broadcast_to((CH, 8, 64))
            sbr = sin_c[bi].ap().unsqueeze(1).broadcast_to((CH, 8, 64))
            tA = scr.ap()[:, 0:512].rearrange("p (h d) -> p h d", d=64)
            tB = scr.ap()[:, 512:1024].rearrange("p (h d) -> p h d", d=64)
            rope_done = rope_emit(kn_v, kr_v, cb, sbr, tA, tB, norm_done)
            rope_cons[bi] = rope_done
            yield
            v_wait("vs", max(rope_done, bdone))
            bbv = bhat.ap().unsqueeze(2).broadcast_to((CH, 8, HD))
            kbv = scr.ap()[:, 1024:2048].rearrange("p (h d) -> p h d", h=8)
            kb_done = v_op(lambda e: e.tensor_tensor(kbv, kr_v, bbv, ALU.mult))
            v_wait("ts", trk2["tbf_free"][bi])
            v_op(lambda e, bi=bi: e.tensor_copy(t_bf2[bi][:, 0:1024], kr_flat), inc=False)
            v_wait("vs", kb_done)
            casts_done = v_op(lambda e, bi=bi: e.tensor_copy(t_bf2[bi][:, 1024:2048], kb_t))
            yield
            kkt_free, spill_free = trk2["fbf_free"][bi]
            for g in range(4):
                half = g % 2
                iskb = g // 2
                src = t_bf2[bi][:, iskb * 1024 + half * 512: iskb * 1024 + half * 512 + 512]
                for q4 in range(4):
                    t_wait("vs", casts_done)
                    p.W("tensor", *trk2["pst_free"])
                    t_wait("ts", 0)
                    transpose(ps_t2[:, q4 * 128:(q4 + 1) * 128],
                              src[:, q4 * 128:(q4 + 1) * 128])
                tpd = p.cnt["ts"]
                v_wait("ts", tpd)
                v_wait("ts", kkt_free)
                p.W("vector", "spk", spill_free)
                dst = fbf2[bi][:, iskb * 1024 + half * 512: iskb * 1024 + half * 512 + 512]
                trk2["pst_free"] = ("vs", v_op(lambda e, d=dst: e.tensor_copy(d, ps_t2[:, :])))
                yield
            trk2["ready"][bi] = (p.cnt["vs"], p.cnt["ss"])

        def drain(g):
            if g is None:
                return
            for _ in g:
                pass

        def chain_and_seq(c, heads0, gen):
            bi = c % 2

            def fill():
                if gen is not None:
                    try:
                        next(gen)
                    except StopIteration:
                        pass

            rdy_v, rdy_s = trk2["ready"][bi]
            kf = fbf2[bi][:, 0:1024]
            kbf = fbf2[bi][:, 1024:2048]
            p.W("sync", "vs", rdy_v)
            p.W("sync", "spk", trk["spill_kbf"])
            trk["spill_kbf"] = sy_dma(
                kbf_s.rearrange("h p nc n -> p h nc n")[:, heads0:heads0 + 8, c, :],
                kbf.rearrange("p (h n) -> p h n", h=8), sem="spk")
            t_wait("vs", rdy_v)
            for k4 in range(4):
                p.W("tensor", *trk2["chain_free"][k4])
            for h in range(8):
                hs = slice(h * CH, (h + 1) * CH)
                mm(raw_slot(h), kf[:, hs], kbf[:, hs], start=True, stop=True)
                mm(T_slot(h), kbf[:, hs], kf[:, hs], start=True, stop=True,
                   inc=(h == 7))
            kkt = p.cnt["ts"]
            trk2["fbf_free"][bi] = (kkt, trk["spill_kbf"])
            fill()
            v_wait("ts", kkt)
            for k4 in range(4):
                po = chain_ps[k4]
                v_op(lambda e, k4=k4, po=po: e.tensor_tensor(
                    Araw_all.ap()[:, k4 * 256:(k4 + 1) * 256].rearrange("p (b n) -> p b n", n=CH),
                    po.ap()[:, 0:256].rearrange("p (b n) -> p b n", n=CH),
                    trilm_b.ap().unsqueeze(1).broadcast_to((CH, 2, CH)),
                    ALU.mult), inc=False)
                v_op(lambda e, k4=k4, po=po: e.tensor_tensor(
                    AT_all.ap()[:, k4 * 256:(k4 + 1) * 256].rearrange("p (b n) -> p b n", n=CH),
                    po.ap()[:, 256:512].rearrange("p (b n) -> p b n", n=CH),
                    trium_b.ap().unsqueeze(1).broadcast_to((CH, 2, CH)),
                    ALU.mult))
            amasks = v_op(lambda e: e.tensor_scalar(nAT_all[:, :], AT_all[:, :], -1.0,
                                                    None, ALU.mult))
            for k4 in range(4):
                trk2["chain_free"][k4] = ("vs", amasks)
            fill()
            v_wait("vs", rdy_v)
            rv_dst = R_all[0].ap().rearrange("p (h n) -> p h n", n=256)
            v_op(lambda e, bi=bi: e.tensor_copy(
                rv_dst[:, :, 0:128],
                v_bf2[bi].ap().rearrange("p (h n) -> p h n", n=128)), inc=False)
            rinit = v_op(lambda e, bi=bi: e.tensor_copy(
                rv_dst[:, :, 128:256],
                t_bf2[bi].ap()[:, 0:1024].rearrange("p (h n) -> p h n", n=128)))
            trk2["vbf_free"][bi] = rinit
            rcur = 0
            for j in range(7):
                pj, pn = j % 2, (j + 1) % 2
                for w in range(2):
                    if j == 0:
                        t_wait("vs", amasks)
                        t_wait("vs", rinit)
                    for hh in range(4):
                        h = w * 4 + hh
                        hs = slice(h * CH, (h + 1) * CH)
                        po = chain_ps[hh]
                        p.W("tensor", *trk2["chain_free"][hh])
                        if j == 0:
                            praw_v, pT_v, facT = Araw_all[:, hs], AT_all[:, hs], nAT_all[:, hs]
                        else:
                            base = h * 256
                            praw_v = PWb[pj][:, base:base + 128]
                            pT_v = PWb[pj][:, base + 128:base + 256]
                            facT = pT_v
                        Rv = R_all[rcur][:, h * 256:(h + 1) * 256]
                        if j < 6:
                            mm(po[:, 0:128], pT_v, praw_v, start=True, stop=True)
                            mm(po[:, 128:256], praw_v, pT_v, start=True, stop=True)
                        mm(po[:, 256:512], ident_b[:, :], Rv, start=True, stop=False)
                        mm(po[:, 256:512], facT, Rv, start=False, stop=True,
                           inc=(hh == 3))
                    gmm = p.cnt["ts"]
                    fill()
                    v_wait("ts", gmm)
                    s_wait("ts", gmm)
                    for hh in range(4):
                        h = w * 4 + hh
                        po = chain_ps[hh]
                        dstP = PWb[pn][:, h * 256:(h + 1) * 256]
                        dstR = R_all[1 - rcur][:, h * 256:(h + 1) * 256]
                        if hh < 2:
                            if j < 6:
                                v_op(lambda e, d=dstP, po=po: e.tensor_copy(d, po[:, 0:256]),
                                     inc=False)
                            ev_ = v_op(lambda e, d=dstR, po=po: e.tensor_copy(d, po[:, 256:512]))
                            trk2["chain_free"][hh] = ("vs", ev_)
                        else:
                            if j < 6:
                                s_op(lambda e, d=dstP, po=po: e.copy(d, po[:, 0:256]),
                                     inc=False)
                            ev_ = s_op(lambda e, d=dstR, po=po: e.copy(d, po[:, 256:512]))
                            trk2["chain_free"][hh] = ("ss", ev_)
                rcur = 1 - rcur
            last_r_v = p.cnt["vs"]
            last_r_s = p.cnt["ss"]
            v_wait("ss", last_r_s)
            u_cp = v_op(lambda e, rc=rcur: e.tensor_copy(
                U_all.ap().rearrange("p (h n) -> p h n", n=128),
                R_all[rc].ap().rearrange("p (h n) -> p h n", n=256)[:, :, 0:128]))
            for g in range(2):
                for q4 in range(4):
                    h = g * 4 + q4
                    t_wait("vs", last_r_v)
                    t_wait("ss", last_r_s)
                    p.W("tensor", *trk2["pst_free"])
                    transpose(ps_t2[:, q4 * 128:(q4 + 1) * 128],
                              R_all[rcur][:, h * 256 + 128: h * 256 + 256])
                tpd = p.cnt["ts"]
                v_wait("ts", tpd)
                trk2["pst_free"] = ("vs", v_op(lambda e, g=g: e.tensor_copy(
                    Wf_all[:, g * 512:(g + 1) * 512], ps_t2[:, :])))
            wf_done = p.cnt["vs"]
            drain(gen)
            t_wait("vs", wf_done)
            t_wait("vs", seqw["s_upd"])
            t_wait("vs", trk["pp_evac"][0])
            t_wait("vs", trk["pp_evac"][1])
            for h in range(8):
                pb = pp[h // 4]
                mm(pb[:, (h % 4) * 128:(h % 4) * 128 + 128], Wf_all[:, h * CH:(h + 1) * CH],
                   S_bf[:, h * HD:(h + 1) * HD], start=True, stop=True, inc=(h == 7))
            psv_done = p.cnt["ts"]
            v_wait("ts", psv_done)
            v_wait("spv", trk["spill_vn"])
            v_op(lambda e: e.tensor_tensor(vn_buf[:, 0:512], U_all[:, 0:512],
                                           pp[0][:, :], ALU.subtract), inc=False)
            vn_done = v_op(lambda e: e.tensor_tensor(vn_buf[:, 512:1024], U_all[:, 512:1024],
                                                     pp[1][:, :], ALU.subtract))
            trk["pp_evac"][0] = vn_done
            trk["pp_evac"][1] = vn_done
            p.W("sync", "vs", vn_done)
            trk["spill_vn"] = sy_dma(
                vnew_s.rearrange("h p nc n -> p h nc n")[:, heads0:heads0 + 8, c, :],
                vn_buf.ap().rearrange("p (h n) -> p h n", h=8), sem="spv")
            t_wait("vs", vn_done)
            p.W("tensor", *trk2["chain_free"][2])
            p.W("tensor", *trk2["chain_free"][3])
            for h in range(8):
                prb = ps_r[h // 4]
                mm(prb[:, (h % 4) * 128:(h % 4) * 128 + 128],
                   t_bf2[bi][:, 1024 + h * HD: 1024 + (h + 1) * HD],
                   vn_buf[:, h * HD:(h + 1) * HD], start=True, stop=True, inc=(h == 7))
            trk2["tbf_free"][bi] = p.cnt["ts"]
            v_wait("ts", p.cnt["ts"])
            v_op(lambda e: e.tensor_tensor(S_f32[:, 0:512], S_f32[:, 0:512],
                                           ps_r[0][:, :], ALU.add), inc=False)
            sadd = v_op(lambda e: e.tensor_tensor(S_f32[:, 512:1024], S_f32[:, 512:1024],
                                                  ps_r[1][:, :], ALU.add))
            trk2["chain_free"][2] = ("vs", sadd)
            trk2["chain_free"][3] = ("vs", sadd)
            v_wait("vs", sadd)
            seqw["s_upd"] = v_op(lambda e: e.tensor_copy(S_bf[:, :], S_f32[:, :]))

        seqw = {"s_upd": 0}
        for hgrp in range(2):
            heads0 = hgrp * 8
            col0 = hgrp * 1024
            barrier()
            lw1 = gp_dma(wres.ap().rearrange("p (dc n) -> p dc n", n=2048)[:, :, 0:1024],
                         wk.rearrange("(dc p) n -> p dc n", p=128)[:, :, col0:col0 + 1024], sem="wld")
            lw2 = gp_dma(wres.ap().rearrange("p (dc n) -> p dc n", n=2048)[:, :, 1024:2048],
                         wv.rearrange("(dc p) n -> p dc n", p=128)[:, :, col0:col0 + 1024], sem="wld")
            lw3 = gp_dma(wb_res.ap().rearrange("p (dc n) -> p dc n", n=8),
                         wb.rearrange("(dc p) n -> p dc n", p=128)[:, :, hgrp * 8:hgrp * 8 + 8], sem="wld")
            v_op(lambda e: e.memset(S_f32[:, :], 0.0), inc=False)
            sz = v_op(lambda e: e.memset(S_bf[:, :], 0.0))
            seqw["s_upd"] = sz
            g0 = stage1(0, [("wld", lw3)])
            drain(g0)
            for c in range(nch):
                gen = stage1(c + 1, []) if c + 1 < nch else None
                chain_and_seq(c, heads0, gen)

    barrier()
    astack.close()


    ot_dbg = dram("ot_dbg", (HD, H * OWN), BF16, kind="ExternalOutput").ap() if dbg_phase_a else None
    bstack = None
    if "B" in phases:
        cur["t"], cur["v"], cur["s"] = "ts2", "vs2", "ss2"
        bstack = contextlib.ExitStack()
        bstack.__enter__()

        def bsbuf(name, shape, dtype):
            return bstack.enter_context(nc.sbuf_tensor(name, shape, dtype))

        def bpsum(name, shape, dtype=F32):
            return bstack.enter_context(nc.psum_tensor(name, shape, dtype))

        otstack = contextlib.ExitStack()
        otstack.__enter__()
        OT_acc = otstack.enter_context(nc.sbuf_tensor("OT_acc", [HD, H * OWN], BF16))
        qfb = [bsbuf(f"qfb{i}", [HD, NOC * CH], BF16) for i in range(2)]
        kbb = [bsbuf(f"kbb{i}", [HD, NCH * CH], BF16) for i in range(2)]
        vnb = [bsbuf(f"vnb{i}", [CH, NCH * HD], BF16) for i in range(2)]
        PTb = [bsbuf(f"PTb{i}", [128, 1024], BF16) for i in range(2)]
        rrow = bsbuf("rrow", [1, TOK_TILE], F32)
        rbc = bsbuf("rbc", [128, TOK_TILE], F32)
        ST_ps = [bpsum(f"ST_ps{i}", [128, 1024], F32) for i in range(2)]
        OT_ps = [bpsum("OT_ps0", [128, TOK_TILE], F32)]
        OT_ps.append(OT_ps[0])
        dn_ps = [bpsum("dn_ps0", [128, TOK_TILE], F32)]
        dn_ps.append(dn_ps[0])
        bc_ps2 = bpsum("bc_ps2", [128, TOK_TILE], F32)
        ones_sq = bsbuf("ones_sq", [128, 128], BF16)
        v_op(lambda e: e.memset(ones_sq[:, :], 1.0))

        barrier()
        hb_cons = [0, 0]
        st_free = [0, 0]
        pt_free = [0, 0]
        ot_evac = [0, 0]
        dn_evac = [0, 0]
        bcb_evac = 0
        ldh = [0, 0]

        def load_head(hh):
            sl = hh % 2
            p.W("sync", "ts2", hb_cons[sl])
            sy_dma(qfb[sl].ap()[:, 0:noc * CH],
                   qf_s[hh, :, 0:noc].rearrange("p nc n -> p (nc n)"), sem=f"hb{sl}")
            sy_dma(kbb[sl].ap()[:, 0:nch * CH],
                   kbf_s[hh, :, 0:nch].rearrange("p nc n -> p (nc n)"), sem=f"hb{sl}")
            ldh[sl] = sy_dma(vnb[sl].ap()[:, 0:nch * HD],
                             vnew_s[hh, :, 0:nch].rearrange("p nc n -> p (nc n)"), sem=f"hb{sl}")

        if dbg_zero_ot:
            v_op(lambda e: e.memset(OT_acc[:, :], 0.0))
        load_head(0)
        load_head(1)
        for h in range(H):
            sl = h % 2
            t_wait(f"hb{sl}", ldh[sl])
            for j in range(nqt):
                nkb = 8 * j + 8
                for kk in range(0, nkb, 2):
                    pb2 = (kk // 2) % 2
                    p.W("tensor", "ss2", st_free[pb2])
                    mm(ST_ps[pb2][:, 0:512], kbb[sl][:, kk * CH:(kk + 1) * CH],
                       qfb[sl][:, j * 512:(j + 1) * 512], start=True, stop=True)
                    smm = mm(ST_ps[pb2][:, 512:1024], kbb[sl][:, (kk + 1) * CH:(kk + 2) * CH],
                             qfb[sl][:, j * 512:(j + 1) * 512], start=True, stop=True, inc=True)
                    s_wait("ts2", smm)
                    p.W("scalar", "ts2", pt_free[pb2])
                    st_free[pb2] = s_op(lambda e, pb2=pb2, h=h: e.activation(
                        PTb[pb2][:, :], ST_ps[pb2][:, :], AF.Exp,
                        bias=Mneg[:, h:h + 1], scale=SCALE))
                    pt_ready_s = st_free
                    pt_ready_s = st_free[pb2]
                    if kk >= 8 * j:
                        m = kk - 8 * j
                        v_wait("ss2", pt_ready_s)
                        v_op(lambda e, pb2=pb2, m=m: e.tensor_tensor(
                            PTb[pb2][:, 0:512], PTb[pb2][:, 0:512],
                            cmask_b[:, m * 512:(m + 1) * 512], ALU.mult))
                        mdone = v_op(lambda e, pb2=pb2, m=m: e.tensor_tensor(
                            PTb[pb2][:, 512:1024], PTb[pb2][:, 512:1024],
                            cmask_b[:, (m + 1) * 512:(m + 2) * 512], ALU.mult))
                        t_wait("vs2", mdone)
                    else:
                        t_wait("ss2", pt_ready_s)
                    if kk == 0:
                        t_wait("vs2", ot_evac[j % 2])
                        t_wait("ss2", dn_evac[j % 2])
                    mm(OT_ps[j % 2][:, :], vnb[sl][:, kk * HD:(kk + 1) * HD],
                       PTb[pb2][:, 0:512], start=(kk == 0), stop=False)
                    mm(OT_ps[j % 2][:, :], vnb[sl][:, (kk + 1) * HD:(kk + 2) * HD],
                       PTb[pb2][:, 512:1024], start=False, stop=(kk + 2 == nkb))
                    mm(dn_ps[j % 2][:, :], ones_sq[:, :],
                       PTb[pb2][:, 0:512], start=(kk == 0), stop=False)
                    pvmm = mm(dn_ps[j % 2][:, :], ones_sq[:, :],
                              PTb[pb2][:, 512:1024], start=False, stop=(kk + 2 == nkb),
                              inc=True)
                    pt_free[pb2] = pvmm
                s_wait("ts2", pvmm)
                s_op(lambda e, j=j: e.activation(rrow[:, :], dn_ps[j % 2][0:1, :], AF.Ln))
                dn_evac[j % 2] = s_op(lambda e: e.activation(rrow[:, :], rrow[:, :],
                                                             AF.Exp, scale=-1.0))
                t_wait("ss2", dn_evac[j % 2])
                t_wait("vs2", bcb_evac)
                bmm = mm(bc_ps2[:, :], ones_row_f[0:1, :], rrow[0:1, :],
                         start=True, stop=True, inc=True)
                v_wait("ts2", bmm)
                bcb_evac = v_op(lambda e: e.tensor_copy(rbc[:, :], bc_ps2[:, :]))
                ot_evac[j % 2] = v_op(lambda e, h=h, j=j: e.tensor_tensor(
                    OT_acc[:, h * OWN + j * 512: h * OWN + (j + 1) * 512],
                    OT_ps[j % 2][:, :], rbc[:, :], ALU.mult))
            hb_cons[sl] = p.cnt["ts2"]
            if h + 2 < H:
                load_head(h + 2)
        if dbg_phase_a:
            p.W("sync", "vs2", p.cnt["vs2"])
            sy_dma(ot_dbg, OT_acc[:, :], sem="sph")
        barrier()
        bstack.close()

    if "C" in phases:
        cur["t"], cur["v"], cur["s"] = "ts2", "vs2", "ss2"
        cstack = contextlib.ExitStack()
        cstack.__enter__()

        def csbuf(name, shape, dtype):
            return cstack.enter_context(nc.sbuf_tensor(name, shape, dtype))

        def cpsum(name, shape, dtype=F32):
            return cstack.enter_context(nc.psum_tensor(name, shape, dtype))

        ld_wo = gp_dma(wres.ap().rearrange("p (ac n) -> p ac n", n=2048),
                       wo.rearrange("(ac p) n -> p ac n", p=128), sem="cwg")
        g_res = csbuf("g_res", [128, DC * NE], F32)
        iota16 = csbuf("iota16", [128, NE], F32)
        iotatok = csbuf("iotatok", [128, NE], U32)
        xrs = [csbuf(f"xrs{i}", [128, 512], F32) for i in range(2)]
        hpre = csbuf("hpre", [128, DC * 512], F32)
        hbf_s = csbuf("hbf_s", [128, 2 * 512], BF16)
        sqbf_s = csbuf("sqbf_s", [128, 2 * 512], BF16)
        r1 = csbuf("r1", [1, 512], F32)
        r2 = csbuf("r2", [1, 512], F32)
        mu2 = csbuf("mu2", [1, 512], F32)
        rb = csbuf("rb", [128, 512], F32)
        mb_ = csbuf("mb_", [128, 512], F32)
        gs = csbuf("gs", [128, NE], F32)
        mx8 = csbuf("mx8", [128, 8], F32)
        mi8 = csbuf("mi8", [128, 8], U32)
        i1f = csbuf("i1f", [128, 2], F32)
        wnorm = csbuf("wnorm", [128, 3], F32)
        msk1 = csbuf("msk1", [128, NE], F32)
        msk2 = csbuf("msk2", [128, NE], F32)
        Mbf = csbuf("Mbf", [128, NE], BF16)
        rloc = csbuf("rloc", [128, NE], F32)
        tilebase = csbuf("tilebase", [1, NE], F32)
        csum = csbuf("csum", [1, NE], F32)
        offs = csbuf("offs", [128, 4], F32)
        selt = csbuf("selt", [128, NE], F32)
        offu = csbuf("offu", [128, 2], U32)
        fill_u = csbuf("fill_u", [128, 64], U32)
        fill_f = csbuf("fill_f", [128, 64], F32)
        h1trow = csbuf("h1trow", [128, D], F32)

        Y_ps = [cpsum(f"Y_ps{i}", [128, 512], F32) for i in range(2)]
        S1_ps = cpsum("S1_ps", [1, 512], F32)
        S2_ps = cpsum("S2_ps", [1, 512], F32)
        BC_ps = cpsum("BC_ps", [128, 512], F32)
        G_ps = cpsum("G_ps", [128, NE], F32)
        TP_ps = cpsum("TP_ps", [128, 512], F32)

        cregs = {}

        def _alloc_bc(e):
            cregs["bc_idx"] = e.alloc_register("bc_idx")
            cregs["bc_tok"] = e.alloc_register("bc_tok")
            e.reg_mov(cregs["bc_idx"], NE * CAP - 1)
            return e.reg_mov(cregs["bc_tok"], OWN - 1)

        p.E("gpsimd", _alloc_bc)
        ld_g = sy_dma(g_res.ap().rearrange("p (dc n) -> p dc n", n=NE),
                      gwT.rearrange("(dc p) n -> p dc n", p=128), sem="cw")
        sy_dma(iota16[:, :], iota16_in, sem="cw")
        ld_i = sy_dma(iotatok[:, :], iotatok_in, sem="cw")
        t_wait("cw", ld_i)
        t_wait("cwg", ld_wo)
        v_wait("cw", ld_i)
        s_wait("cw", ld_i)
        v_op(lambda e: e.memset(fill_u[:, :], 100000), inc=False)
        fz = v_op(lambda e: e.memset(fill_f[:, :], 0.0))
        p.W("sync", "vs2", fz)
        pre1 = sy_dma(idx_s.rearrange("(p b) one -> p (b one)", p=128), fill_u[:, :], sem="pre")
        pre2 = sy_dma(wg_s.rearrange("(p b) one -> p (b one)", p=128),
                      fill_f[:, :], sem="pre")

        zb = v_op(lambda e: e.memset(tilebase[:, :], 0.0))

        y_evac = [0, 0]
        stat_guard = [0, 0]
        stats_state = {"smm": 0}
        h1_spill = 0
        gmm_last = 0
        gsig_last = 0
        stats_read = 0
        bcc_evac = 0
        scat_cnt = 0
        tp_evac = 0
        h1t_spill = 0
        xr_cons = [0, 0]
        for t in range(NQT):
            p.W("vector", "h1f0", h1_spill)
            pend = None
            for dco in range(DC):
                xsl = dco % 2
                p.W("sync", "vs2", xr_cons[xsl])
                ldx2 = sy_dma(xrs[xsl][:, :],
                              xq.rearrange("(dc p) n -> p dc n", p=128)[:, dco,
                                           t * 512:(t + 1) * 512], sem=f"xr{xsl}")
                yp = Y_ps[dco % 2]
                t_wait("vs2", y_evac[dco % 2])
                for ac in range(H):
                    mm(yp[:, :], wres[:, ac * 2048 + dco * 128: ac * 2048 + dco * 128 + 128],
                       OT_acc[:, ac * OWN + t * 512: ac * OWN + (t + 1) * 512],
                       start=(ac == 0), stop=(ac == H - 1), inc=(ac == H - 1))
                wmm = p.cnt["ts2"]
                if pend is not None:
                    pend()
                v_wait("ts2", wmm)
                v_wait(f"xr{xsl}", ldx2)
                y_evac[dco % 2] = v_op(lambda e, dco=dco, yp=yp, xsl=xsl: e.tensor_tensor(
                    hpre[:, dco * 512:(dco + 1) * 512], yp[:, :],
                    xrs[xsl][:, :], ALU.add))
                xr_cons[xsl] = y_evac[dco % 2]
                s_wait("vs2", y_evac[dco % 2])
                s_wait("ts2", stat_guard[dco % 2])
                s_op(lambda e, dco=dco: e.copy(
                    hbf_s[:, (dco % 2) * 512:(dco % 2) * 512 + 512],
                    hpre[:, dco * 512:(dco + 1) * 512]), inc=False)
                sq_d = s_op(lambda e, dco=dco: e.activation(
                    sqbf_s[:, (dco % 2) * 512:(dco % 2) * 512 + 512],
                    hpre[:, dco * 512:(dco + 1) * 512], AF.Square))

                def pend(dco=dco, sq_d=sq_d):
                    par = dco % 2
                    if dco == 0:
                        t_wait("vs2", stats_read)
                    t_wait("ss2", sq_d)
                    mm(S1_ps[0:1, :], ones_col[:, 0:1],
                       hbf_s[:, par * 512:par * 512 + 512],
                       start=(dco == 0), stop=(dco == DC - 1))
                    stats_state["smm"] = mm(
                        S2_ps[0:1, :], ones_col[:, 0:1],
                        sqbf_s[:, par * 512:par * 512 + 512],
                        start=(dco == 0), stop=(dco == DC - 1), inc=True)
                    stat_guard[par] = p.cnt["ts2"]
            pend()
            smm = stats_state["smm"]
            v_wait("ts2", smm)
            v_op(lambda e: e.tensor_scalar(r1[:, :], S1_ps[0:1, :], 1.0 / D, None, ALU.mult))
            stats_read = v_op(lambda e: e.tensor_scalar(r2[:, :], S2_ps[0:1, :], 1.0 / D, None, ALU.mult))
            s_wait("vs2", stats_read)
            sq_mu = s_op(lambda e: e.activation(mu2[:, :], r1[:, :], AF.Square))
            v_wait("ss2", sq_mu)
            vv = v_op(lambda e: e.tensor_tensor(r2[:, :], r2[:, :], mu2[:, :], ALU.subtract))
            s_wait("vs2", vv)
            s_op(lambda e: e.activation(r2[:, :], r2[:, :], AF.Ln, bias=epsln_b[:, :]))
            rstd_d = s_op(lambda e: e.activation(r2[:, :], r2[:, :], AF.Exp, scale=-0.5))
            v_wait("ss2", rstd_d)
            murs_d = v_op(lambda e: e.tensor_tensor(r1[:, :], r1[:, :], r2[:, :], ALU.mult))
            t_wait("vs2", murs_d)
            t_wait("vs2", bcc_evac)
            bmm = mm(BC_ps[:, :], ones_row_f[0:1, :], r2[0:1, :], start=True, stop=True, inc=True)
            v_wait("ts2", bmm)
            e1 = v_op(lambda e: e.tensor_copy(rb[:, :], BC_ps[:, :]))
            t_wait("vs2", e1)
            bmm = mm(BC_ps[:, :], ones_row_f[0:1, :], r1[0:1, :], start=True, stop=True, inc=True)
            v_wait("ts2", bmm)
            bcc_evac = v_op(lambda e: e.tensor_copy(mb_[:, :], BC_ps[:, :]))
            p.W("vector", "h1f0", h1_spill)
            v_wait("ts2", gmm_last)
            for dco in range(DC):
                v_op(lambda e, dco=dco: e.tensor_tensor(
                    hpre[:, dco * 512:(dco + 1) * 512], hpre[:, dco * 512:(dco + 1) * 512],
                    rb[:, :], ALU.mult))
                nrm = v_op(lambda e, dco=dco: e.tensor_tensor(
                    hpre[:, dco * 512:(dco + 1) * 512], hpre[:, dco * 512:(dco + 1) * 512],
                    mb_[:, :], ALU.subtract))
            p.W("sync", "vs2", nrm)
            h1_spill = sy_dma(h1_s.rearrange("dc p n -> p dc n")[:, :, t * 512:(t + 1) * 512],
                              hpre.ap().rearrange("p (dc n) -> p dc n", n=512), sem="h1f0")
            for sub in range(4):
                for dco in range(DC):
                    t_wait("vs2", nrm)
                    if dco % 4 == 0:
                        t_wait("vs2", tp_evac)
                    transpose(TP_ps[:, (dco % 4) * 128:(dco % 4) * 128 + 128],
                              hpre[:, dco * 512 + sub * 128: dco * 512 + sub * 128 + 128],
                              ident=ident_f)
                    if dco % 4 == 3:
                        tpm = p.cnt["ts2"]
                        v_wait("ts2", tpm)
                        p.W("vector", "h1t", h1t_spill)
                        p.W("vector", "h1tb", p.cnt["h1tb"])
                        tp_evac = v_op(lambda e, dco=dco, sub=sub: e.tensor_copy(
                            h1trow[:, (dco - 3) * 128:(dco + 1) * 128], TP_ps[:, :]))
                p.W("sync", "vs2", tp_evac)
                sy_dma(h1tokf_s[t * 512 + sub * 128: t * 512 + (sub + 1) * 128, :],
                       h1trow[:, :], sem="h1t")
                h1t_spill = p.cnt["h1t"]
                p.W("gpsimd", "vs2", tp_evac)
                gp_dma(h1tok_s[t * 512 + sub * 128: t * 512 + (sub + 1) * 128, :],
                       h1trow[:, :], sem="h1tb")
            for sub in range(4):
                t_wait("vs2", nrm)
                t_wait("ss2", gsig_last)
                t_wait("vs2", trk.get("gps_read", 0))
                for dco in range(DC):
                    mm(G_ps[:, :], hpre[:, dco * 512 + sub * 128: dco * 512 + sub * 128 + 128],
                       g_res[:, dco * NE:(dco + 1) * NE],
                       start=(dco == 0), stop=(dco == DC - 1), inc=(dco == DC - 1))
                gmm_last = p.cnt["ts2"]
                s_wait("ts2", gmm_last)
                gsig_last = s_op(lambda e: e.activation(gs[:, :], G_ps[:, :], AF.Sigmoid))
                v_wait("ss2", gsig_last)
                p.W("vector", "scx", scat_cnt)
                v_op(lambda e: e.max(mx8[:, :], gs[:, :]))
                v_op(lambda e: e.max_index(mi8[:, :], mx8[:, :], gs[:, :]))
                v_op(lambda e: e.tensor_copy(i1f[:, 0:2], mi8[:, 0:2]))
                sden = v_op(lambda e: e.tensor_tensor(wnorm[:, 0:1], mx8[:, 0:1], mx8[:, 1:2], ALU.add))
                s_wait("vs2", sden)
                s_op(lambda e: e.activation(wnorm[:, 1:2], wnorm[:, 0:1], AF.Ln))
                srcp = s_op(lambda e: e.activation(wnorm[:, 1:2], wnorm[:, 1:2], AF.Exp, scale=-1.0))
                v_wait("ss2", srcp)
                v_op(lambda e: e.tensor_tensor(wnorm[:, 0:1], mx8[:, 0:1], wnorm[:, 1:2], ALU.mult))
                v_op(lambda e: e.tensor_tensor(wnorm[:, 2:3], mx8[:, 1:2], wnorm[:, 1:2], ALU.mult))
                v_op(lambda e: e.tensor_scalar(msk1[:, :], iota16[:, :], i1f[:, 0:1], None, ALU.is_equal))
                mm2 = v_op(lambda e: e.tensor_scalar(msk2[:, :], iota16[:, :], i1f[:, 1:2], None, ALU.is_equal))
                v_wait("vs2", mm2)
                mjd = v_op(lambda e: e.tensor_tensor(rloc[:, :], msk1[:, :], msk2[:, :], ALU.add))
                v_wait("vs2", mjd)
                mbd = v_op(lambda e: e.tensor_copy(Mbf[:, :], rloc[:, :]))
                t_wait("vs2", mbd)
                t_wait("vs2", p.cnt["vs2"])
                rmm = mm(G_ps[:, :], trium_b[:, :], Mbf[:, :], start=True, stop=True, inc=True)
                bbm = mm(BC_ps[:, 0:NE], ones_row_f[0:1, :], tilebase[0:1, :],
                         start=True, stop=True, inc=True)
                csm = mm(TP_ps[0:1, 0:NE], ones_col[:, 0:1], Mbf[:, :], start=True, stop=True, inc=True)
                v_wait("ts2", csm)
                v_op(lambda e: e.tensor_copy(rloc[:, :], G_ps[:, :]), inc=False)
                rgd = v_op(lambda e: e.tensor_tensor(rloc[:, :], rloc[:, :], BC_ps[:, 0:NE], ALU.add))
                trk["gps_read"] = rgd
                v_op(lambda e: e.tensor_tensor(csum[:, :], TP_ps[0:1, 0:NE], tilebase[:, :], ALU.add))
                tbd = v_op(lambda e: e.tensor_copy(tilebase[:, :], csum[:, :]))
                for ch in range(2):
                    mskc = msk1 if ch == 0 else msk2
                    v_op(lambda e, mskc=mskc: e.tensor_tensor(selt[:, :], rloc[:, :], mskc[:, :], ALU.mult))
                    rs = v_op(lambda e: e.tensor_reduce(
                        offs.ap()[:, 2:3], selt.ap()[:, :], AX.X, ALU.add))
                    v_wait("vs2", rs)
                    v_op(lambda e: e.tensor_scalar(offs[:, 3:4], offs[:, 2:3], 512.0, None, ALU.is_ge))
                    v_op(lambda e, ch=ch: e.scalar_tensor_tensor(
                        offs[:, ch:ch + 1], offs[:, 3:4], 100000.0, offs[:, 2:3],
                        ALU.mult, ALU.add))
                    v_op(lambda e, ch=ch: e.scalar_tensor_tensor(
                        offs[:, ch:ch + 1], i1f[:, ch:ch + 1], float(CAP), offs[:, ch:ch + 1],
                        ALU.mult, ALU.add))
                    cvt = v_op(lambda e, ch=ch: e.tensor_copy(offu[:, ch:ch + 1], offs[:, ch:ch + 1]))
                p.W("gpsimd", "vs2", cvt)
                p.W("gpsimd", "pre", pre2)
                p.W("gpsimd", "scx", scat_cnt)
                tokcol = iotatok[:, t * 4 + sub: t * 4 + sub + 1]
                gp_dma_scatter = None
                for ch in range(2):
                    wcol = wnorm[:, 0:1] if ch == 0 else wnorm[:, 2:3]
                    p.inc("scx", 16)
                    p.E("gpsimd", lambda e, ch=ch, tokcol=tokcol: e.indirect_dma_start(
                        idx_s, IndirectOffsetOnAxis(offu[:, ch:ch + 1], 0),
                        tokcol, None, bounds_check=cregs["bc_idx"], oob_is_err=False
                    ).then_inc(L["scx"], 16))
                    p.inc("scx", 16)
                    p.E("gpsimd", lambda e, ch=ch, wcol=wcol: e.indirect_dma_start(
                        wg_s, IndirectOffsetOnAxis(offu[:, ch:ch + 1], 0),
                        wcol, None, bounds_check=cregs["bc_idx"], oob_is_err=False
                    ).then_inc(L["scx"], 16))
                scat_cnt = p.cnt["scx"]
        barrier()
        cstack.close()
        if "B" in phases:
            otstack.close()
        wstack.close()

        c2stack = contextlib.ExitStack()
        c2stack.__enter__()

        def c2sbuf(name, shape, dtype):
            return c2stack.enter_context(nc.sbuf_tensor(name, shape, dtype))

        def c2psum(name, shape, dtype=F32):
            return c2stack.enter_context(nc.psum_tensor(name, shape, dtype))

        h1bf = c2sbuf("h1bf", [128, DC * 1024], BF16)
        yT = c2sbuf("yT", [128, DC * 1024], F32)
        sa_bf = c2sbuf("sa_bf", [128, IC * 1024], BF16)
        hbuf = c2sbuf("hbuf", [128, IC * 1024], BF16)
        wseg = [c2sbuf(f"wseg{i}", [128, DC * 512], BF16) for i in range(2)]
        ytrow = c2sbuf("ytrow", [128, D], F32)
        M_ps = [c2psum(f"M_ps{i}", [128, 1024], F32) for i in range(2)]
        TPC_ps = c2psum("TPC_ps", [128, 512], F32)

        I_SEGS = [(0, 4), (4, 4), (8, 3)]

        wseg_cons = [0, 0]
        ldw_cnt = [0, 0]
        mps_free = [("ss2", 0), ("ss2", 0)]
        sa_done = {}
        h_done = {}
        h1b_cons = 0
        ytp_evac = 0
        yt_spill = 0

        def wload_c2(slot, cols, src_ap):
            qeng = "sync" if slot == 0 else "gpsimd"
            p.W(qeng, "ts2", wseg_cons[slot])
            fn = sy_dma if slot == 0 else gp_dma
            return fn(wseg[slot][:, 0:cols], src_ap, sem=f"ws{slot}")

        for r in range(2):
            p.W("gpsimd", "ts2", h1b_cons)
            ldh1 = gp_dma(h1bf.ap().rearrange("p (dc n) -> p dc n", n=1024),
                          h1_s.rearrange("dc p n -> p dc n")[:, :, r * 1024:(r + 1) * 1024],
                          sem="h1b")
            for kk, kind in enumerate(("s0", "s1")):
                for si, (ic0, icn) in enumerate(I_SEGS):
                    slot = si % 2
                    cols = 16 * icn * 128
                    ldw_cnt[slot] = wload_c2(
                        slot, cols,
                        sw1p[:, kk * W13_COLS + OFF1[si]: kk * W13_COLS + OFF1[si] + cols])
                    for ici in range(icn):
                        ic = ic0 + ici
                        mp = M_ps[ic % 2]
                        t_wait(f"ws{slot}", ldw_cnt[slot])
                        if kind == "s0" and ic == 0:
                            t_wait("h1b", ldh1)
                        p.W("tensor", *mps_free[ic % 2])
                        for dc in range(DC):
                            for tt in range(2):
                                mm(mp[:, tt * 512:(tt + 1) * 512],
                                   wseg[slot][:, (dc * icn + ici) * 128: (dc * icn + ici) * 128 + 128],
                                   h1bf[:, dc * 1024 + tt * 512: dc * 1024 + tt * 512 + 512],
                                   start=(dc == 0), stop=(dc == DC - 1),
                                   inc=(dc == DC - 1 and tt == 1))
                        am = p.cnt["ts2"]
                        s_wait("ts2", am)
                        sa_done[ic] = s_op(lambda e, ic=ic, mp=mp: e.activation(
                            sa_bf[:, ic * 1024:(ic + 1) * 1024], mp[:, :], AF.Silu))
                        mps_free[ic % 2] = ("ss2", sa_done[ic])
                    wseg_cons[slot] = p.cnt["ts2"]
                for si, (ic0, icn) in enumerate(I_SEGS):
                    slot = si % 2
                    cols = 16 * icn * 128
                    ldw_cnt[slot] = wload_c2(
                        slot, cols,
                        sw3p[:, kk * W13_COLS + OFF1[si]: kk * W13_COLS + OFF1[si] + cols])
                    for ici in range(icn):
                        ic = ic0 + ici
                        mp = M_ps[ic % 2]
                        t_wait(f"ws{slot}", ldw_cnt[slot])
                        p.W("tensor", *mps_free[ic % 2])
                        for dc in range(DC):
                            for tt in range(2):
                                mm(mp[:, tt * 512:(tt + 1) * 512],
                                   wseg[slot][:, (dc * icn + ici) * 128: (dc * icn + ici) * 128 + 128],
                                   h1bf[:, dc * 1024 + tt * 512: dc * 1024 + tt * 512 + 512],
                                   start=(dc == 0), stop=(dc == DC - 1),
                                   inc=(dc == DC - 1 and tt == 1))
                        bm = p.cnt["ts2"]
                        v_wait("ts2", bm)
                        v_wait("ss2", sa_done[ic])
                        h_done[ic] = v_op(lambda e, ic=ic, mp=mp: e.tensor_tensor(
                            hbuf[:, ic * 1024:(ic + 1) * 1024],
                            sa_bf[:, ic * 1024:(ic + 1) * 1024], mp[:, :], ALU.mult))
                        mps_free[ic % 2] = ("vs2", h_done[ic])
                    wseg_cons[slot] = p.cnt["ts2"]
                h_all = max(h_done.values())
                D_SEGS = [(0, 4), (4, 4), (8, 4), (12, 4)]
                for si, (dc0, dcn) in enumerate(D_SEGS):
                    slot = si % 2
                    cols = 11 * dcn * 128
                    ldw_cnt[slot] = wload_c2(
                        slot, cols,
                        sw2p[:, kk * W13_COLS + OFF2[si]: kk * W13_COLS + OFF2[si] + cols])
                    for dci in range(dcn):
                        dco = dc0 + dci
                        mp = M_ps[dco % 2]
                        t_wait(f"ws{slot}", ldw_cnt[slot])
                        p.W("tensor", *mps_free[dco % 2])
                        t_wait("vs2", h_all)
                        for ic in range(IC):
                            for tt in range(2):
                                mm(mp[:, tt * 512:(tt + 1) * 512],
                                   wseg[slot][:, (ic * dcn + dci) * 128: (ic * dcn + dci) * 128 + 128],
                                   hbuf[:, ic * 1024 + tt * 512: ic * 1024 + tt * 512 + 512],
                                   start=(ic == 0), stop=(ic == IC - 1),
                                   inc=(ic == IC - 1 and tt == 1))
                        ym = p.cnt["ts2"]
                        v_wait("ts2", ym)
                        if kind == "s0":
                            yv = v_op(lambda e, dco=dco, mp=mp: e.tensor_copy(
                                yT[:, dco * 1024:(dco + 1) * 1024], mp[:, :]))
                        else:
                            yv = v_op(lambda e, dco=dco, mp=mp: e.tensor_tensor(
                                yT[:, dco * 1024:(dco + 1) * 1024],
                                yT[:, dco * 1024:(dco + 1) * 1024], mp[:, :], ALU.add))
                        mps_free[dco % 2] = ("vs2", yv)
                    wseg_cons[slot] = p.cnt["ts2"]
            h1b_cons = p.cnt["ts2"]
            ysh_done = p.cnt["vs2"]
            for tt in range(2):
                for sub in range(4):
                    for dco in range(DC):
                        t_wait("vs2", ysh_done)
                        if dco % 4 == 0:
                            t_wait("vs2", ytp_evac)
                        transpose(TPC_ps[:, (dco % 4) * 128:(dco % 4) * 128 + 128],
                                  yT[:, dco * 1024 + tt * 512 + sub * 128:
                                     dco * 1024 + tt * 512 + sub * 128 + 128],
                                  ident=ident_f)
                        if dco % 4 == 3:
                            tpm = p.cnt["ts2"]
                            v_wait("ts2", tpm)
                            p.W("vector", "ytw", yt_spill)
                            ytp_evac = v_op(lambda e, dco=dco: e.tensor_copy(
                                ytrow[:, (dco - 3) * 128:(dco + 1) * 128], TPC_ps[:, :]))
                    p.W("sync", "vs2", ytp_evac)
                    row0 = r * 1024 + tt * 512 + sub * 128
                    yt_spill = sy_dma(y_tok_s[row0:row0 + 128, :], ytrow[:, :], sem="ytw")
        barrier()
        c2stack.close()

        c3stack = contextlib.ExitStack()
        c3stack.__enter__()

        def c3sbuf(name, shape, dtype):
            return c3stack.enter_context(nc.sbuf_tensor(name, shape, dtype))

        def c3psum(name, shape, dtype=F32):
            return c3stack.enter_context(nc.psum_tensor(name, shape, dtype))

        NSUB = CAP // 128
        gidx = c3sbuf("gidx", [128, NSUB], U32)
        wgrow = c3sbuf("wgrow", [1, CAP], F32)
        xe_tok = [c3sbuf(f"xe_tok{i}", [128, D], BF16) for i in range(2)]
        xef = c3sbuf("xef", [128, DC * CAP], BF16)
        sa2 = c3sbuf("sa2", [128, IC * CAP], BF16)
        h2 = c3sbuf("h2", [128, IC * CAP], BF16)
        wseg2 = [c3sbuf(f"wseg2_{i}", [128, DC * 512], BF16) for i in range(3)]
        outf = c3sbuf("outf", [128, DC * CAP], F32)
        strow = c3sbuf("strow", [128, D], F32)
        ME_ps = [c3psum(f"ME_ps{i}", [128, 512], F32) for i in range(2)]
        WG_ps = c3psum("WG_ps", [128, CAP], F32)
        PTX_ps = c3psum("PTX_ps", [128, 512], BF16)
        TPO_ps = c3psum("TPO_ps", [128, 512], F32)

        wseg2_cons = [0, 0, 0]
        ldw2 = [("ws0", 0)] * 3
        segctr = [0]

        def wload_c3(cols, src_ap):
            s = segctr[0]
            segctr[0] += 1
            slot = s % 3
            qeng = "sync" if s % 2 == 0 else "gpsimd"
            sem = "ws0" if s % 2 == 0 else "ws1"
            p.W(qeng, "ts2", wseg2_cons[slot])
            fn = sy_dma if s % 2 == 0 else gp_dma
            ldw2[slot] = (sem, fn(wseg2[slot][:, 0:cols], src_ap, sem=sem))
            return slot
        me_free = [("ss2", 0), ("ss2", 0)]
        ptx_free = ("vs2", 0)
        tpo_free = ("vs2", 0)
        sca_cnt = 0
        gi_cons = 0
        xe_cons = [0, 0]
        wg_read = 0
        st_cons = 0

        p.W("sync", "scx", p.cnt["scx"])
        p.W("sync", "ytw", p.cnt["ytw"])
        p.W("gpsimd", "scx", p.cnt["scx"])
        p.W("gpsimd", "h1tb", p.cnt["h1tb"])
        for ei in range(nex):
            p.W("sync", "gat", p.cnt["gat"])
            p.W("sync", "sca", p.cnt["sca"])
            for sub in range(NSUB):
                sy_dma(gidx[:, sub:sub + 1],
                       idx_s[ei * CAP + sub * 128: ei * CAP + (sub + 1) * 128, :],
                       sem="gld")
            p.W("sync", "ts2", wg_read)
            wld_ = sy_dma(wgrow[:, :],
                          wg_s[ei * CAP:(ei + 1) * CAP, 0:1].squeeze(1).unsqueeze(0),
                          sem="gld")
            for sub in range(NSUB):
                xp = sub % 2
                p.W("gpsimd", "gld", wld_)
                p.W("gpsimd", "ts2", xe_cons[xp])
                gat = p.inc("gat", 16)
                p.E("gpsimd", lambda e, sub=sub, xp=xp: e.indirect_dma_start(
                    xe_tok[xp][:, :], None, h1tok_s,
                    IndirectOffsetOnAxis(gidx[:, sub:sub + 1], 0),
                    bounds_check=cregs["bc_tok"], oob_is_err=False).then_inc(L["gat"], 16))
                for dco in range(DC):
                    t_wait("gat", gat)
                    if dco % 4 == 0:
                        p.W("tensor", *ptx_free)
                    transpose(PTX_ps[:, (dco % 4) * 128:(dco % 4) * 128 + 128],
                              xe_tok[xp][:, dco * 128:(dco + 1) * 128])
                    if dco % 4 == 3:
                        tpm = p.cnt["ts2"]
                        v_wait("ts2", tpm)
                        ptx_free = ("vs2", v_op(lambda e, dco=dco, sub=sub: e.tensor_copy(
                            xef.ap().rearrange("p (dc s) -> p dc s", s=CAP)[
                                :, dco - 3:dco + 1, sub * 128:(sub + 1) * 128],
                            PTX_ps.ap().rearrange("p (b n) -> p b n", n=128))))
                xe_cons[xp] = p.cnt["ts2"]
            xef_ready = p.cnt["vs2"]
            gi_cons = p.cnt["ts2"]
            t_wait("gld", wld_)
            t_wait("vs2", wg_read_v if ei > 0 else 0)
            mm(WG_ps[:, 0:512], ones_row_f[0:1, :], wgrow[0:1, 0:512], start=True, stop=True,
               inc=True)
            wgb = p.cnt["ts2"]
            wg_read = wgb
            for si, (ic0, icn) in enumerate(I_SEGS):
                cols = 16 * icn * 128
                slot = wload_c3(cols, w1p[ei][:, OFF1[si]:OFF1[si] + cols])
                for ici in range(icn):
                    ic = ic0 + ici
                    mp = ME_ps[ic % 2]
                    t_wait(*ldw2[slot])
                    v_wait2 = xef_ready
                    t_wait("vs2", v_wait2)
                    p.W("tensor", *me_free[ic % 2])
                    for dc in range(DC):
                        mm(mp[:, :], wseg2[slot][:, (dc * icn + ici) * 128: (dc * icn + ici) * 128 + 128],
                           xef[:, dc * CAP: dc * CAP + 512],
                           start=(dc == 0), stop=(dc == DC - 1), inc=(dc == DC - 1))
                    am = p.cnt["ts2"]
                    s_wait("ts2", am)
                    sa_d = s_op(lambda e, ic=ic, mp=mp: e.activation(
                        sa2[:, ic * CAP:(ic + 1) * CAP], mp[:, :], AF.Silu))
                    me_free[ic % 2] = ("ss2", sa_d)
                wseg2_cons[slot] = p.cnt["ts2"]
            sa_all_s = p.cnt["ss2"]
            h_last = 0
            for si, (ic0, icn) in enumerate(I_SEGS):
                cols = 16 * icn * 128
                slot = wload_c3(cols, w3p[ei][:, OFF1[si]:OFF1[si] + cols])
                for ici in range(icn):
                    ic = ic0 + ici
                    mp = ME_ps[ic % 2]
                    t_wait(*ldw2[slot])
                    p.W("tensor", *me_free[ic % 2])
                    for dc in range(DC):
                        mm(mp[:, :], wseg2[slot][:, (dc * icn + ici) * 128: (dc * icn + ici) * 128 + 128],
                           xef[:, dc * CAP: dc * CAP + 512],
                           start=(dc == 0), stop=(dc == DC - 1), inc=(dc == DC - 1))
                    bm = p.cnt["ts2"]
                    v_wait("ts2", bm)
                    v_wait("ss2", sa_all_s)
                    v_op(lambda e, ic=ic, mp=mp: e.tensor_tensor(
                        h2[:, ic * CAP:(ic + 1) * CAP],
                        sa2[:, ic * CAP:(ic + 1) * CAP], mp[:, :], ALU.mult))
                    hd = v_op(lambda e, ic=ic: e.tensor_tensor(
                        h2[:, ic * CAP:(ic + 1) * CAP],
                        h2[:, ic * CAP:(ic + 1) * CAP], WG_ps[:, 0:512], ALU.mult))
                    h_last = hd
                    me_free[ic % 2] = ("vs2", hd)
                wseg2_cons[slot] = p.cnt["ts2"]
            wg_read_v = h_last
            D_SEGS = [(0, 4), (4, 4), (8, 4), (12, 4)]
            of_last = 0
            for si, (dc0, dcn) in enumerate(D_SEGS):
                cols = 11 * dcn * 128
                slot = wload_c3(cols, w2p[ei][:, OFF2[si]:OFF2[si] + cols])
                for dci in range(dcn):
                    dco = dc0 + dci
                    mp = ME_ps[dco % 2]
                    t_wait(*ldw2[slot])
                    p.W("tensor", *me_free[dco % 2])
                    t_wait("vs2", h_last)
                    for ic in range(IC):
                        mm(mp[:, :], wseg2[slot][:, (ic * dcn + dci) * 128: (ic * dcn + dci) * 128 + 128],
                           h2[:, ic * CAP: ic * CAP + 512],
                           start=(ic == 0), stop=(ic == IC - 1), inc=(ic == IC - 1))
                    ym = p.cnt["ts2"]
                    v_wait("ts2", ym)
                    ofd = v_op(lambda e, dco=dco, mp=mp: e.tensor_copy(
                        outf[:, dco * CAP: dco * CAP + 512], mp[:, :]))
                    of_last = ofd
                    me_free[dco % 2] = ("vs2", ofd)
                wseg2_cons[slot] = p.cnt["ts2"]
            for sub in range(NSUB):
                for dco in range(DC):
                    t_wait("vs2", of_last)
                    if dco % 4 == 0:
                        p.W("tensor", *tpo_free)
                    transpose(TPO_ps[:, (dco % 4) * 128:(dco % 4) * 128 + 128],
                              outf[:, dco * CAP + sub * 128: dco * CAP + sub * 128 + 128],
                              ident=ident_f)
                    if dco % 4 == 3:
                        tpm = p.cnt["ts2"]
                        v_wait("ts2", tpm)
                        p.W("vector", "sca", sca_cnt)
                        tpo_free = ("vs2", v_op(lambda e, dco=dco: e.tensor_copy(
                            strow[:, (dco - 3) * 128:(dco + 1) * 128], TPO_ps[:, :])))
                p.W("gpsimd", "vs2", tpo_free[1])
                p.W("gpsimd", "sca", sca_cnt)
                p.inc("sca", 16)
                p.E("gpsimd", lambda e, sub=sub: e.indirect_dma_start(
                    y_tok_s, IndirectOffsetOnAxis(gidx[:, sub:sub + 1], 0),
                    strow[:, :], None, bounds_check=cregs["bc_tok"], oob_is_err=False,
                    compute_op=ALU.add).then_inc(L["sca"], 16))
                sca_cnt = p.cnt["sca"]
                st_cons = sca_cnt
        barrier()
        c3stack.close()

        c4stack = contextlib.ExitStack()
        c4stack.__enter__()
        yrow = [c4stack.enter_context(nc.sbuf_tensor(f"yrow{i}", [128, D], F32)) for i in range(2)]
        hrow = [c4stack.enter_context(nc.sbuf_tensor(f"hrow{i}", [128, D], F32)) for i in range(2)]
        orow = c4stack.enter_context(nc.sbuf_tensor("orow", [128, D], F32))
        sqrow = c4stack.enter_context(nc.sbuf_tensor("sqrow", [128, D], F32))
        strow2 = c4stack.enter_context(nc.sbuf_tensor("strow2", [128, 4], F32))
        epsc = c4stack.enter_context(nc.sbuf_tensor("epsc", [128, 1], F32))
        ld_e = sy_dma(epsc[:, :], epslncol_in, sem="cw")
        s_wait("cw", ld_e)
        yr_cons = [0, 0]
        out_cnt = 0
        sq_c = 0
        for tile in range(OWN // 128):
            sl = tile % 2
            p.W("sync", "vs2", yr_cons[sl])
            l1 = sy_dma(yrow[sl][:, :], y_tok_s[tile * 128:(tile + 1) * 128, :], sem=f"h1f{sl}")
            l2 = sy_dma(hrow[sl][:, :], h1tokf_s[tile * 128:(tile + 1) * 128, :], sem=f"h1f{sl}")
            v_wait(f"h1f{sl}", l2)
            radd = v_op(lambda e, sl=sl: e.tensor_tensor(yrow[sl][:, :], yrow[sl][:, :],
                                                         hrow[sl][:, :], ALU.add))
            yr_cons[sl] = radd
            v_op(lambda e, sl=sl: e.tensor_reduce(
                strow2.ap()[:, 0:1], yrow[sl].ap(), AX.X, ALU.add), inc=False)
            s_wait("vs2", radd)
            s_wait("ss2", sq_c)
            sq_c = s_op(lambda e, sl=sl: e.activation(sqrow[:, :], yrow[sl][:, :], AF.Square))
            v_wait("ss2", sq_c)
            rd2 = v_op(lambda e: e.tensor_reduce(
                strow2.ap()[:, 1:2], sqrow.ap(), AX.X, ALU.add))
            v_wait("vs2", rd2)
            v_op(lambda e: e.tensor_scalar(strow2[:, 0:1], strow2[:, 0:1], 1.0 / D, None, ALU.mult), inc=False)
            m2d = v_op(lambda e: e.tensor_scalar(strow2[:, 1:2], strow2[:, 1:2], 1.0 / D, None, ALU.mult))
            s_wait("vs2", m2d)
            smu = s_op(lambda e: e.activation(strow2[:, 2:3], strow2[:, 0:1], AF.Square))
            v_wait("ss2", smu)
            vd = v_op(lambda e: e.tensor_tensor(strow2[:, 1:2], strow2[:, 1:2], strow2[:, 2:3],
                                                ALU.subtract))
            s_wait("vs2", vd)
            s_op(lambda e: e.activation(strow2[:, 1:2], strow2[:, 1:2], AF.Ln, bias=epsc[:, :]))
            rsd = s_op(lambda e: e.activation(strow2[:, 1:2], strow2[:, 1:2], AF.Exp, scale=-0.5))
            v_wait("ss2", rsd)
            p.W("vector", "out", out_cnt)
            v_op(lambda e, sl=sl: e.tensor_scalar(orow[:, :], yrow[sl][:, :], strow2[:, 0:1],
                                                  None, ALU.subtract), inc=False)
            od = v_op(lambda e: e.tensor_scalar(orow[:, :], orow[:, :], strow2[:, 1:2],
                                                None, ALU.mult))
            p.W("sync", "vs2", od)
            out_cnt = sy_dma(outR[tile * 128:(tile + 1) * 128, :], orow[:, :], sem="out")
        c4stack.close()

    for s_ in ("lsy", "cs0", "cs1", "spv", "spk", "spq", "hb0", "hb1", "cw", "cwg",
               "xr0", "xr1", "h1f0", "h1f1", "out", "sph"):
        p.W("sync", s_, p.cnt[s_])
    for s_ in ("wld", "xc0", "xc1", "h1b", "ws0", "ws1", "cwg"):
        p.W("gpsimd", s_, p.cnt[s_])
    for s_ in ("vs", "vs2"):
        p.W("vector", s_, p.cnt[s_])
    for s_ in ("ts", "ts2"):
        p.W("tensor", s_, p.cnt[s_])
    for s_ in ("ss", "ss2"):
        p.W("scalar", s_, p.cnt[s_])

    blk = stack.enter_context(nc.Block())
    p.emit(blk)
    stack.close()
    return nc



def kernel(**inputs):
    nc = build_program()
    in_maps = [stage_inputs(c, **inputs) for c in range(8)]
    res = run_bass_kernel_spmd(nc, in_maps, core_ids=list(range(8)))
    out = np.zeros((4, N, D), np.float32)
    for c in range(8):
        b, hg = c // 2, c % 2
        rows = _own_rows(hg)
        out[b, rows, :] = np.asarray(res.results[c]["outR"])
    return out


# version 09
# speedup vs baseline: 1.2215x; 1.2215x over previous
import contextlib
import numpy as np
import ml_dtypes
import concourse.bass as bass
import concourse.mybir as mybir
from concourse.bass_utils import run_bass_kernel_spmd
from concourse.bass_core_rust import IndirectOffsetOnAxis

F32 = mybir.dt.float32
BF16 = mybir.dt.bfloat16
U32 = mybir.dt.uint32
AF = mybir.ActivationFunctionType
ALU = mybir.AluOpType
AX = mybir.AxisListType

D = 2048
N = 4096
H = 16
HD = 128
CH = 128
NCH = N // CH
OWN = 2048
NOC = OWN // CH
DC = D // 128
SCALE = HD ** (-0.5)
EPS_L2 = 1e-6
EPS_LN = 1e-5
NE = 16
I_FF = 1408
IC = 11
SI_FF = 2816
SIC = 22
TOK_TILE = 512
NQT = OWN // TOK_TILE
POWS = [2, 4, 8, 16, 32, 64]



def _own_rows(hg):
    return np.concatenate([np.arange(512) + 512 * (2 * j + hg) for j in range(4)])


def _rope_tables():
    inv = 1.0 / (10000.0 ** (np.arange(0, HD, 2, dtype=np.float32) / HD))
    th = np.arange(N, dtype=np.float32)[:, None] * inv[None, :]
    return np.cos(th).astype(np.float32), np.sin(th).astype(np.float32)


def _masks(hg):
    i = np.arange(CH)
    trilm = (i[:, None] > i[None, :]).astype(ml_dtypes.bfloat16)
    trium = (i[:, None] < i[None, :]).astype(ml_dtypes.bfloat16)
    tk = np.arange(128)[:, None]
    tq = np.arange(TOK_TILE)[None, :]
    slots = []
    for m in range(8):
        if hg == 0:
            msk = (128 * m + tk <= tq) if m < 4 else np.zeros((128, TOK_TILE), bool)
        else:
            msk = np.ones((128, TOK_TILE), bool) if m < 4 else (128 * (m - 4) + tk <= tq)
        slots.append(msk.astype(ml_dtypes.bfloat16))
    return trilm, trium, np.concatenate(slots, axis=1)


I_SEGS_H = [(0, 4), (4, 4), (8, 3)]
D_SEGS_H = [(0, 4), (4, 4), (8, 4), (12, 4)]
OFF1 = [0, 8192, 16384]
W13_COLS = 22528
OFF2 = [0, 5632, 11264, 16896]


def _pack13(w):
    blocks = []
    for ic0, icn in I_SEGS_H:
        b = w[ic0 * 128:(ic0 + icn) * 128, :]
        b = b.T.reshape(16, 128, icn * 128).transpose(1, 0, 2).reshape(128, -1)
        blocks.append(b)
    return np.concatenate(blocks, axis=1)


def _pack2(w):
    blocks = []
    for dc0, dcn in D_SEGS_H:
        b = w[dc0 * 128:(dc0 + dcn) * 128, :]
        b = b.T.reshape(11, 128, dcn * 128).transpose(1, 0, 2).reshape(128, -1)
        blocks.append(b)
    return np.concatenate(blocks, axis=1)


def stage_inputs(core, L0_attn_x, L0_attn_Wq, L0_attn_Wk, L0_attn_Wv, L0_attn_Wo,
                 L0_attn_Wb, L0_ffn_gate_w, L0_ffn_w1, L0_ffn_w2, L0_ffn_w3,
                 L0_ffn_sw1, L0_ffn_sw2, L0_ffn_sw3):
    b, hg = core // 2, core % 2
    x = np.asarray(L0_attn_x)
    rows = _own_rows(hg)
    cos_t, sin_t = _rope_tables()
    trilm, trium, cmask = _masks(hg)
    bf = ml_dtypes.bfloat16
    w1 = np.asarray(L0_ffn_w1)
    w2 = np.asarray(L0_ffn_w2)
    w3 = np.asarray(L0_ffn_w3)
    sw1 = np.asarray(L0_ffn_sw1)
    sw2 = np.asarray(L0_ffn_sw2)
    sw3 = np.asarray(L0_ffn_sw3)
    sw1p = np.concatenate([_pack13(sw1[k * 1408:(k + 1) * 1408, :]) for k in range(2)], axis=1)
    sw3p = np.concatenate([_pack13(sw3[k * 1408:(k + 1) * 1408, :]) for k in range(2)], axis=1)
    sw2p = np.concatenate([_pack2(sw2[:, k * 1408:(k + 1) * 1408]) for k in range(2)], axis=1)
    w1p = np.stack([_pack13(w1[e]) for e in range(NE)])
    w3p = np.stack([_pack13(w3[e]) for e in range(NE)])
    w2p = np.stack([_pack2(w2[e]) for e in range(NE)])
    return {
        "xT": np.ascontiguousarray(x[b].T).astype(np.float32, copy=False),
        "xq": np.ascontiguousarray(x[b][rows].T).astype(np.float32, copy=False),
        "wq": np.ascontiguousarray(L0_attn_Wq),
        "wk": np.ascontiguousarray(L0_attn_Wk),
        "wv": np.ascontiguousarray(L0_attn_Wv),
        "wb": np.ascontiguousarray(L0_attn_Wb),
        "wo": np.ascontiguousarray(L0_attn_Wo),
        "gwT": np.ascontiguousarray(np.asarray(L0_ffn_gate_w).T),
        "w1p": np.ascontiguousarray(w1p),
        "w2p": np.ascontiguousarray(w2p),
        "w3p": np.ascontiguousarray(w3p),
        "sw1p": np.ascontiguousarray(sw1p),
        "sw2p": np.ascontiguousarray(sw2p),
        "sw3p": np.ascontiguousarray(sw3p),
        "cos_t": cos_t, "sin_t": sin_t,
        "qcos": np.ascontiguousarray(cos_t[rows]),
        "qsin": np.ascontiguousarray(sin_t[rows]),
        "trilm": trilm, "trium": trium, "cmask": cmask,
        "ident_bf": np.eye(128, dtype=bf),
        "ident_f32": np.eye(128, dtype=np.float32),
        "eps_l2": np.full((128, 1), EPS_L2, np.float32),
        "eps_ln": np.full((1, 1), EPS_LN, np.float32),
        "iota16_in": np.broadcast_to(np.arange(NE, dtype=np.float32), (128, NE)).copy(),
        "iotatok_in": (np.arange(16, dtype=np.uint32)[None, :] * 128
                       + np.arange(128, dtype=np.uint32)[:, None]).copy(),
        "epslncol": np.full((128, 1), EPS_LN, np.float32),
    }



class Prog:
    ENGS = ("sync", "gpsimd", "tensor", "vector", "scalar")

    def __init__(self, nc, stack):
        self.nc = nc
        self.stack = stack
        self.q = {e: [] for e in self.ENGS}
        self.sems = {}
        self.cnt = {}
        self.lastwait = {e: {} for e in self.ENGS}

    def sem(self, name):
        h = self.stack.enter_context(self.nc.semaphore(name))
        self.sems[name] = h
        self.cnt[name] = 0

    def inc(self, name, n=1):
        self.cnt[name] += n
        return self.cnt[name]

    def E(self, eng, fn):
        self.q[eng].append(fn)

    def W(self, eng, sem, val):
        if val <= 0:
            return
        lw = self.lastwait[eng]
        if lw.get(sem, -1) >= val:
            return
        lw[sem] = val
        h = self.sems[sem]
        self.q[eng].append(lambda e, h=h, v=val: e.wait_ge(h, v))

    def emit(self, block):
        dec = {"sync": block.sync, "gpsimd": block.gpsimd, "tensor": block.tensor,
               "vector": block.vector, "scalar": block.scalar}
        for eng in self.ENGS:
            ops = self.q[eng]
            for i in range(0, len(ops), 4000):
                sub = ops[i:i + 4000]
                dec[eng](lambda e, sub=sub: [f(e) for f in sub])


def build_program(nch=NCH, noc=NOC, nqt=NQT, nex=NE, dbg_phase_a=False,
                  dbg_zero_ot=False, phases=("A", "B", "C")):
    nc = bass.Bass("TRN2", target_bir_lowering=False, debug=False)
    stack = contextlib.ExitStack()
    stack.__enter__()

    def dram(name, shape, dtype, kind="ExternalInput"):
        return nc.dram_tensor(name, list(shape), dtype, kind=kind)

    xT = dram("xT", (D, N), F32).ap()
    xq = dram("xq", (D, OWN), F32).ap()
    wq = dram("wq", (D, D), F32).ap()
    wk = dram("wk", (D, D), F32).ap()
    wv = dram("wv", (D, D), F32).ap()
    wb = dram("wb", (D, H), F32).ap()
    wo = dram("wo", (D, D), F32).ap()
    gwT = dram("gwT", (D, NE), F32).ap()
    w1p = dram("w1p", (NE, 128, W13_COLS), BF16).ap()
    w2p = dram("w2p", (NE, 128, W13_COLS), BF16).ap()
    w3p = dram("w3p", (NE, 128, W13_COLS), BF16).ap()
    sw1p = dram("sw1p", (128, 2 * W13_COLS), BF16).ap()
    sw2p = dram("sw2p", (128, 2 * W13_COLS), BF16).ap()
    sw3p = dram("sw3p", (128, 2 * W13_COLS), BF16).ap()
    cos_t = dram("cos_t", (N, 64), F32).ap()
    sin_t = dram("sin_t", (N, 64), F32).ap()
    qcos = dram("qcos", (OWN, 64), F32).ap()
    qsin = dram("qsin", (OWN, 64), F32).ap()
    trilm = dram("trilm", (CH, CH), BF16).ap()
    trium = dram("trium", (CH, CH), BF16).ap()
    cmask = dram("cmask", (128, 8 * TOK_TILE), BF16).ap()
    ident_bf = dram("ident_bf", (128, 128), BF16).ap()
    ident_f32 = dram("ident_f32", (128, 128), F32).ap()
    eps_l2 = dram("eps_l2", (128, 1), F32).ap()
    eps_ln = dram("eps_ln", (1, 1), F32).ap()
    iota16_in = dram("iota16_in", (128, NE), F32).ap()
    iotatok_in = dram("iotatok_in", (128, NE), U32).ap()
    epslncol_in = dram("epslncol", (128, 1), F32).ap()

    outR = dram("outR", (OWN, D), F32, kind="ExternalOutput").ap()

    spill_kind = "ExternalOutput" if dbg_phase_a else "Internal"
    vnew_s = dram("vnew_s", (H, 128, NCH, HD), BF16, kind=spill_kind).ap()
    kbf_s = dram("kbf_s", (H, 128, NCH, CH), BF16, kind=spill_kind).ap()
    qf_s = dram("qf_s", (H, 128, NOC, CH), BF16, kind=spill_kind).ap()
    mneg_dbg = dram("mneg_dbg", (1, H), F32, kind=spill_kind).ap() if dbg_phase_a else None
    bhat_dbg = dram("bhat_dbg", (2, NCH, CH, 8), F32, kind=spill_kind).ap() if dbg_phase_a else None
    braw_dbg = dram("braw_dbg", (2, NCH, CH, 8), F32, kind=spill_kind).ap() if dbg_phase_a else None
    krt_dbg = dram("krt_dbg", (CH, 1024), F32, kind=spill_kind).ap() if dbg_phase_a else None
    h1_s = dram("h1_s", (DC, 128, OWN), F32, kind="Internal").ap()
    h1tok_s = dram("h1tok_s", (OWN, D), BF16, kind="Internal").ap()
    h1tokf_s = dram("h1tokf_s", (OWN, D), F32, kind="Internal").ap()
    y_tok_s = dram("y_tok_s", (OWN, D), F32, kind="Internal").ap()
    CAP = 384
    idx_s = dram("idx_s", (NE * CAP, 1), U32, kind="Internal").ap()
    wg_s = dram("wg_s", (NE * CAP, 1), F32, kind="Internal").ap()

    p = Prog(nc, stack)
    for s in ("lsy", "wld", "xc0", "xc1", "cs0", "cs1", "spv", "spk", "spq",
              "ts", "vs", "ss", "gp",
              "hb0", "hb1", "cw", "cwg", "xr0", "xr1", "h1b", "h1f0", "h1f1",
              "ws0", "ws1", "out", "ts2", "vs2", "ss2", "sph",
              "scx", "h1t", "h1tb", "ytw", "gld", "gat", "sca", "pre"):
        p.sem(s)
    L = p.sems

    def sbuf(name, shape, dtype):
        return stack.enter_context(nc.sbuf_tensor(name, shape, dtype))

    def psum(name, shape, dtype=F32):
        return stack.enter_context(nc.psum_tensor(name, shape, dtype))

    ident_b = sbuf("ident_b", [128, 128], BF16)
    ident_f = sbuf("ident_f", [128, 128], F32)
    trilm_b = sbuf("trilm_b", [CH, CH], BF16)
    trium_b = sbuf("trium_b", [CH, CH], BF16)
    cmask_b = sbuf("cmask_b", [128, 8 * TOK_TILE], BF16)
    epsl2_b = sbuf("epsl2_b", [128, 1], F32)
    epsln_b = sbuf("epsln_b", [1, 1], F32)
    ones_col = sbuf("ones_col", [128, 1], BF16)
    ones_row = sbuf("ones_row", [1, 128], BF16)
    ones_col_f = sbuf("ones_col_f", [128, 1], F32)
    ones_row_f = sbuf("ones_row_f", [1, 128], F32)

    def sy_dma(out_ap, in_ap, sem="lsy"):
        p.inc(sem, 16)
        p.E("sync", lambda e, o=out_ap, i=in_ap, s=sem: e.dma_start(out=o, in_=i).then_inc(L[s], 16))
        return p.cnt[sem]

    def gp_dma(out_ap, in_ap, sem="wld"):
        p.inc(sem, 16)
        p.E("gpsimd", lambda e, o=out_ap, i=in_ap, s=sem: e.dma_start(out=o, in_=i).then_inc(L[s], 16))
        return p.cnt[sem]

    for dst, src in ((ident_b[:, :], ident_bf), (ident_f[:, :], ident_f32),
                     (trilm_b[:, :], trilm), (trium_b[:, :], trium),
                     (cmask_b[:, :], cmask), (epsl2_b[:, :], eps_l2),
                     (epsln_b[:, :], eps_ln)):
        sy_dma(dst, src)
    const_ld = p.cnt["lsy"]

    cur = {"t": "ts", "v": "vs", "s": "ss"}

    def t_wait(sem, val):
        p.W("tensor", sem, val)

    def v_wait(sem, val):
        p.W("vector", sem, val)

    def s_wait(sem, val):
        p.W("scalar", sem, val)

    def mm(psum_ap, lhsT_ap, rhs_ap, start, stop, inc=False):
        sn = cur["t"]
        if inc:
            p.inc(sn)
        p.E("tensor", lambda e, o=psum_ap, a=lhsT_ap, b=rhs_ap, st=start, sp=stop, i=inc, sn=sn:
            (e.matmul(o, a, b, start=st, stop=sp).then_inc(L[sn], 1) if i
             else e.matmul(o, a, b, start=st, stop=sp)))
        return p.cnt[sn]

    def transpose(psum_ap, in_ap, ident=None):
        sn = cur["t"]
        p.inc(sn)
        idn = ident if ident is not None else ident_b
        p.E("tensor", lambda e, o=psum_ap, a=in_ap, idn=idn, sn=sn:
            e.transpose(o, a, idn[:, :]).then_inc(L[sn], 1))
        return p.cnt[sn]

    def v_op(fn, inc=True):
        sn = cur["v"]
        p.W("vector", sn, p.cnt[sn])
        p.inc(sn)
        p.E("vector", lambda e, f=fn, sn=sn: f(e).then_inc(L[sn], 1))
        return p.cnt[sn]

    def s_op(fn, inc=True):
        sn = cur["s"]
        p.W("scalar", sn, p.cnt[sn])
        p.inc(sn)
        p.E("scalar", lambda e, f=fn, sn=sn: f(e).then_inc(L[sn], 1))
        return p.cnt[sn]

    v_op(lambda e: e.memset(ones_col[:, :], 1.0), inc=False)
    v_op(lambda e: e.memset(ones_row[:, :], 1.0), inc=False)
    v_op(lambda e: e.memset(ones_col_f[:, :], 1.0), inc=False)
    ones_done = v_op(lambda e: e.memset(ones_row_f[:, :], 1.0))

    dwT = sbuf("dwT", [NE, OWN], F32)
    wstack = contextlib.ExitStack()
    wstack.__enter__()
    wres = wstack.enter_context(nc.sbuf_tensor("wres", [128, DC * 2048], BF16, side="right"))
    Mneg = sbuf("Mneg", [128, H], F32)
    braw = sbuf("braw", [CH, 8], F32)
    bhat = sbuf("bhat", [CH, 8], F32)
    krsum = sbuf("krsum", [CH, 8], F32)
    krstd = sbuf("krstd", [CH, 8], F32)
    qrsum = sbuf("qrsum", [CH, H], F32)
    qn_c = sbuf("qn_c", [1, H], F32)
    Mrun = sbuf("Mrun", [1, H], F32)
    Mrow = sbuf("Mrow", [1, H], F32)

    trk = {
        "pp_evac": [0, 0], "ab_evac": 0, "r_evac": [0, 0], "tp_evac": 0, "bc_evac": 0,
        "spill_vn": 0, "spill_kbf": 0, "spill_qf": 0,
    }

    def barrier():
        snap = dict(p.cnt)
        for eng in Prog.ENGS:
            for s_, v_ in snap.items():
                p.W(eng, s_, v_)

    astack = contextlib.ExitStack()
    astack.__enter__()

    def asbuf(name, shape, dtype):
        return astack.enter_context(nc.sbuf_tensor(name, shape, dtype))

    def apsum(name, shape, dtype=F32):
        return astack.enter_context(nc.psum_tensor(name, shape, dtype))

    pp = [apsum(f"pp{i}", [128, 512], F32) for i in range(2)]
    ps_ab = apsum("ps_ab", [128, 512], F32)
    ps_r = [apsum(f"ps_r{i}", [128, 512], F32) for i in range(2)]
    ps_t2q = apsum("ps_t2q", [128, 512], BF16)
    ps_bc = apsum("ps_bc", [128, 512], F32)
    xc_bf = [asbuf(f"xc{i}", [128, DC * CH], BF16) for i in range(2)]
    cos_c = [asbuf(f"cosc{i}", [CH, 64], F32) for i in range(2)]
    sin_c = [asbuf(f"sinc{i}", [CH, 64], F32) for i in range(2)]
    wb_res = asbuf("wb_res", [128, DC * 8], BF16)
    ktk = asbuf("ktk", [CH, 2048], F32)
    krx = asbuf("krx", [CH, 2048], F32)
    scr = asbuf("scr", [CH, 2048], F32)
    t_bf = asbuf("t_bf", [CH, 2048], BF16)
    fbf = asbuf("fbf", [HD, 2048], BF16)
    v_bf = asbuf("v_bf", [CH, 1024], BF16)
    A_bf = asbuf("A_bf", [CH, CH], BF16)
    AT_bf = asbuf("AT_bf", [CH, CH], BF16)
    nAT_bf = asbuf("nAT_bf", [CH, CH], BF16)
    powT = asbuf("powT", [CH, 6 * CH], BF16)
    praw = [asbuf(f"praw{i}", [CH, CH], BF16) for i in range(2)]
    Rbuf = [asbuf(f"Rbuf{i}", [CH, 256], BF16) for i in range(2)]
    U_all = asbuf("U_all", [CH, 8 * HD], BF16)
    Wf_all = asbuf("Wf_all", [HD, 8 * CH], BF16)
    vn_buf = asbuf("vn_buf", [CH, 8 * HD], BF16)
    S_f32 = asbuf("S_f32", [HD, 8 * HD], F32)
    S_bf = asbuf("S_bf", [HD, 8 * HD], BF16)

    def rope_emit(src_view, dst_view, cosb, sinb, tA, tB, after_vs):
        av = tA
        bv = tB
        v_wait("vs", after_vs)
        v_op(lambda e: e.tensor_tensor(av, src_view[:, :, 0:64], cosb, ALU.mult), inc=False)
        d1 = v_op(lambda e: e.tensor_tensor(bv, src_view[:, :, 64:128], sinb, ALU.mult))
        v_wait("vs", d1)
        v_op(lambda e: e.tensor_tensor(dst_view[:, :, 0:64], av, bv, ALU.subtract), inc=False)
        v_op(lambda e: e.tensor_tensor(av, src_view[:, :, 0:64], sinb, ALU.mult), inc=False)
        d2 = v_op(lambda e: e.tensor_tensor(bv, src_view[:, :, 64:128], cosb, ALU.mult))
        v_wait("vs", d2)
        return v_op(lambda e: e.tensor_tensor(dst_view[:, :, 64:128], av, bv, ALU.add))

    def proj_tiles(xbuf, wres_, wwid, tiles, first_waits):
        for ti, (w_off, width, dst) in enumerate(tiles):
            pb = pp[ti % 2]
            for sem, val in first_waits:
                t_wait(sem, val)
            t_wait("vs", trk["pp_evac"][ti % 2])
            for dc in range(DC):
                mm(pb[:, 0:width], xbuf[:, dc * CH:(dc + 1) * CH],
                   wres_[:, dc * wwid + w_off: dc * wwid + w_off + width],
                   start=(dc == 0), stop=(dc == DC - 1), inc=(dc == DC - 1))
            v_wait("ts", p.cnt["ts"])
            trk["pp_evac"][ti % 2] = v_op(
                lambda e, d=dst, pb=pb, w=width: e.tensor_copy(d, pb[:, 0:w]))
        return p.cnt["vs"]

    if "A" in phases:
        t_wait("lsy", const_ld)
        v_wait("lsy", const_ld)
        s_wait("lsy", const_ld)
        ld_wq = gp_dma(wres.ap().rearrange("p (dc n) -> p dc n", n=D),
                       wq.rearrange("(dc p) n -> p dc n", p=128), sem="wld")
        v_op(lambda e: e.memset(Mrun[:, :], 0.0))
        xc_cons = [0, 0]
        rope_cons = [0, 0]

        for c in range(noc):
            bi = c % 2
            p.W("gpsimd", "ts", xc_cons[bi])
            ldx = gp_dma(xc_bf[bi].ap().rearrange("p (dc n) -> p dc n", n=CH),
                         xq.rearrange("(dc p) (nc n) -> p dc nc n", p=128, n=CH)[:, :, c, :],
                         sem=f"xc{bi}")
            p.W("sync", "vs", rope_cons[bi])
            sy_dma(cos_c[bi][:, :], qcos[c * CH:(c + 1) * CH, :], sem=f"cs{bi}")
            lds = sy_dma(sin_c[bi][:, :], qsin[c * CH:(c + 1) * CH, :], sem=f"cs{bi}")
            waits = [(f"xc{bi}", ldx)] + ([("wld", ld_wq)] if c == 0 else [])
            ev = proj_tiles(xc_bf[bi], wres, D,
                            [(t * 512, 512, ktk[:, t * 512:(t + 1) * 512]) for t in range(4)],
                            waits)
            xc_cons[bi] = p.cnt["ts"]
            v_wait(f"cs{bi}", lds)
            qv = ktk.ap().rearrange("p (h d) -> p h d", h=H)
            rv = krx.ap().rearrange("p (h d) -> p h d", h=H)
            cb = cos_c[bi].ap().unsqueeze(1).broadcast_to((CH, H, 64))
            sbr = sin_c[bi].ap().unsqueeze(1).broadcast_to((CH, H, 64))
            tA = scr.ap()[:, 0:1024].rearrange("p (h d) -> p h d", d=64)
            tB = scr.ap()[:, 1024:2048].rearrange("p (h d) -> p h d", d=64)
            rope_done = rope_emit(qv, rv, cb, sbr, tA, tB, ev)
            rope_cons[bi] = rope_done
            cast_done = v_op(lambda e: e.tensor_copy(t_bf[:, :], krx[:, :]))
            s_wait("vs", rope_done)
            sq_done = s_op(lambda e: e.activation(scr[:, :], krx[:, :], AF.Square))
            v_wait("ss", sq_done)
            red = v_op(lambda e: e.tensor_reduce(
                qrsum.ap().unsqueeze(2),
                scr.ap().rearrange("p (h d) -> p h d", h=H), AX.X, ALU.add))
            t_wait("vs", red)
            t_wait("vs", ones_done)
            t_wait("vs", trk["ab_evac"])
            qs_mm = mm(ps_ab[0:1, 0:H], ones_col_f[:, 0:1], qrsum[:, :],
                       start=True, stop=True, inc=True)
            v_wait("ts", qs_mm)
            cp = v_op(lambda e: e.tensor_copy(qn_c[:, :], ps_ab[0:1, 0:H]))
            trk["ab_evac"] = cp
            v_wait("vs", cp)
            v_op(lambda e: e.tensor_tensor(Mrun[:, :], Mrun[:, :], qn_c[:, :], ALU.max))
            v_wait("spq", trk["spill_qf"])
            for g in range(4):
                for q4 in range(4):
                    h = g * 4 + q4
                    t_wait("vs", cast_done)
                    t_wait("vs", trk["tp_evac"])
                    transpose(ps_t2q[:, q4 * 128:(q4 + 1) * 128], t_bf[:, h * HD:(h + 1) * HD])
                tp = p.cnt["ts"]
                v_wait("ts", tp)
                trk["tp_evac"] = v_op(
                    lambda e, g=g: e.tensor_copy(fbf[:, g * 512:(g + 1) * 512], ps_t2q[:, :]))
            p.W("sync", "vs", trk["tp_evac"])
            trk["spill_qf"] = sy_dma(
                qf_s.rearrange("h p nc n -> p h nc n")[:, :, c, :],
                fbf.ap().rearrange("p (h n) -> p h n", h=H), sem="spq")

        s2 = 2.0 * SCALE
        s_wait("vs", p.cnt["vs"])
        mr = s_op(lambda e: e.activation(Mrow[:, :], Mrun[:, :], AF.Sqrt, scale=s2 * s2))
        v_wait("ss", mr)
        mneg_row = v_op(lambda e: e.tensor_scalar(Mrow[:, :], Mrow[:, :], -1.0, None, ALU.mult))
        t_wait("vs", mneg_row)
        t_wait("vs", trk["bc_evac"])
        mb = mm(ps_bc[:, 0:H], ones_row_f[0:1, :], Mrow[0:1, :], start=True, stop=True, inc=True)
        v_wait("ts", mb)
        mneg_done = v_op(lambda e: e.tensor_copy(Mneg[:, :], ps_bc[:, 0:H]))
        trk["bc_evac"] = mneg_done
        if dbg_phase_a:
            p.W("sync", "vs", mneg_done)
            sy_dma(mneg_dbg, Mneg[0:1, :], sem="lsy")

        k_tok = ktk[:, 0:1024]
        kn_v = krx.ap()[:, 0:1024].rearrange("p (h d) -> p h d", h=8)
        kr_v = krx.ap()[:, 1024:2048].rearrange("p (h d) -> p h d", h=8)
        kr_flat = krx[:, 1024:2048]
        ksq = scr[:, 0:1024]
        kb_t = scr[:, 1024:2048]

        v_bf2 = [v_bf, asbuf("v_bfB", [CH, 1024], BF16)]
        t_bf2 = [t_bf, asbuf("t_bfB", [CH, 2048], BF16)]
        fbf2 = [fbf, asbuf("fbfB", [HD, 2048], BF16)]
        Araw_all = asbuf("Araw_all", [CH, 1024], BF16)
        AT_all = asbuf("AT_all", [CH, 1024], BF16)
        nAT_all = asbuf("nAT_all", [CH, 1024], BF16)
        PWb = [asbuf(f"PWb{i}", [CH, 8 * 256], BF16) for i in range(2)]
        R_all = [asbuf(f"R_all{i}", [CH, 8 * 256], BF16) for i in range(2)]
        ps_t2 = ps_t2q

        chain_ps = [ps_ab, ps_bc, ps_r[0], ps_r[1]]

        def raw_slot(h):
            return chain_ps[h // 2][:, (h % 2) * 128:(h % 2) * 128 + 128]

        def T_slot(h):
            return chain_ps[h // 2][:, 256 + (h % 2) * 128: 256 + (h % 2) * 128 + 128]

        def ap_slot(h):
            return chain_ps[h // 2][:, (h % 2) * 256:(h % 2) * 256 + 256]

        trk2 = {
            "chain_free": [("vs", 0)] * 4,
            "pst_free": ("vs", 0),
            "tbf_free": [0, 0],
            "vbf_free": [0, 0],
            "fbf_free": [(0, 0), (0, 0)],
            "ready": [(0, 0), (0, 0)],
        }

        def stage1(c, first_waits):
            bi = c % 2
            p.W("gpsimd", "ts", xc_cons[bi])
            ldx = gp_dma(xc_bf[bi].ap().rearrange("p (dc n) -> p dc n", n=CH),
                         xT.rearrange("(dc p) (nc n) -> p dc nc n", p=128, n=CH)[:, :, c, :],
                         sem=f"xc{bi}")
            p.W("sync", "vs", rope_cons[bi])
            sy_dma(cos_c[bi][:, :], cos_t[c * CH:(c + 1) * CH, :], sem=f"cs{bi}")
            lds = sy_dma(sin_c[bi][:, :], sin_t[c * CH:(c + 1) * CH, :], sem=f"cs{bi}")
            yield
            tiles = [(0, 512, k_tok[:, 0:512]), (512, 512, k_tok[:, 512:1024]),
                     (1024, 512, v_bf2[bi][:, 0:512]), (1536, 512, v_bf2[bi][:, 512:1024])]
            for ti, (w_off, width, dst) in enumerate(tiles):
                pb = pp[ti % 2]
                t_wait(f"xc{bi}", ldx)
                for sem_, val_ in first_waits:
                    t_wait(sem_, val_)
                t_wait("vs", trk["pp_evac"][ti % 2])
                if ti >= 2:
                    t_wait("vs", trk2["vbf_free"][bi])
                for dc in range(DC):
                    mm(pb[:, 0:width], xc_bf[bi][:, dc * CH:(dc + 1) * CH],
                       wres[:, dc * 2048 + w_off: dc * 2048 + w_off + width],
                       start=(dc == 0), stop=(dc == DC - 1), inc=(dc == DC - 1))
                v_wait("ts", p.cnt["ts"])
                trk["pp_evac"][ti % 2] = v_op(
                    lambda e, d=dst, pb=pb, w=width: e.tensor_copy(d, pb[:, 0:w]))
                yield
            pb = pp[0]
            t_wait("vs", trk["pp_evac"][0])
            for dc in range(DC):
                mm(pb[:, 0:8], xc_bf[bi][:, dc * CH:(dc + 1) * CH],
                   wb_res[:, dc * 8: dc * 8 + 8],
                   start=(dc == 0), stop=(dc == DC - 1), inc=(dc == DC - 1))
            v_wait("ts", p.cnt["ts"])
            if dbg_phase_a:
                v_wait("lsy", trk.get("dbg_spill", 0))
            trk["pp_evac"][0] = v_op(lambda e, pb=pb: e.tensor_copy(braw[:, 0:8], pb[:, 0:8]))
            ev = p.cnt["vs"]
            xc_cons[bi] = p.cnt["ts"]
            yield
            s_wait("vs", ev)
            if dbg_phase_a:
                s_wait("lsy", trk.get("dbg_spill", 0))
            bsig = s_op(lambda e: e.activation(bhat[:, :], braw[:, :], AF.Sigmoid))
            v_wait("ss", bsig)
            bdone = v_op(lambda e: e.tensor_scalar(bhat[:, :], bhat[:, :], 2.0, None, ALU.mult))
            s_wait("vs", ev)
            ks_done = s_op(lambda e: e.activation(ksq, k_tok, AF.Square))
            v_wait("ss", ks_done)
            red = v_op(lambda e: e.tensor_reduce(
                krsum.ap()[:, 0:8].unsqueeze(2),
                scr.ap()[:, 0:1024].rearrange("p (h d) -> p h d", h=8), AX.X, ALU.add))
            s_wait("vs", red)
            s_op(lambda e: e.activation(krstd[:, :], krsum[:, :], AF.Ln, bias=epsl2_b[:, :]),
                 inc=False)
            rstd_done = s_op(lambda e: e.activation(krstd[:, :], krstd[:, :], AF.Exp, scale=-0.5))
            v_wait("ss", rstd_done)
            ktv = ktk.ap()[:, 0:1024].rearrange("p (h d) -> p h d", h=8)
            rsv = krstd.ap()[:, 0:8].unsqueeze(2).broadcast_to((CH, 8, HD))
            norm_done = v_op(lambda e: e.tensor_tensor(kn_v, ktv, rsv, ALU.mult))
            v_wait(f"cs{bi}", lds)
            cb = cos_c[bi].ap().unsqueeze(1).broadcast_to((CH, 8, 64))
            sbr = sin_c[bi].ap().unsqueeze(1).broadcast_to((CH, 8, 64))
            tA = scr.ap()[:, 0:512].rearrange("p (h d) -> p h d", d=64)
            tB = scr.ap()[:, 512:1024].rearrange("p (h d) -> p h d", d=64)
            rope_done = rope_emit(kn_v, kr_v, cb, sbr, tA, tB, norm_done)
            rope_cons[bi] = rope_done
            yield
            v_wait("vs", max(rope_done, bdone))
            bbv = bhat.ap().unsqueeze(2).broadcast_to((CH, 8, HD))
            kbv = scr.ap()[:, 1024:2048].rearrange("p (h d) -> p h d", h=8)
            kb_done = v_op(lambda e: e.tensor_tensor(kbv, kr_v, bbv, ALU.mult))
            v_wait("ts", trk2["tbf_free"][bi])
            v_op(lambda e, bi=bi: e.tensor_copy(t_bf2[bi][:, 0:1024], kr_flat), inc=False)
            v_wait("vs", kb_done)
            casts_done = v_op(lambda e, bi=bi: e.tensor_copy(t_bf2[bi][:, 1024:2048], kb_t))
            yield
            kkt_free, spill_free = trk2["fbf_free"][bi]
            for g in range(4):
                half = g % 2
                iskb = g // 2
                src = t_bf2[bi][:, iskb * 1024 + half * 512: iskb * 1024 + half * 512 + 512]
                for q4 in range(4):
                    t_wait("vs", casts_done)
                    p.W("tensor", *trk2["pst_free"])
                    t_wait("ts", 0)
                    transpose(ps_t2[:, q4 * 128:(q4 + 1) * 128],
                              src[:, q4 * 128:(q4 + 1) * 128])
                tpd = p.cnt["ts"]
                v_wait("ts", tpd)
                v_wait("ts", kkt_free)
                p.W("vector", "spk", spill_free)
                dst = fbf2[bi][:, iskb * 1024 + half * 512: iskb * 1024 + half * 512 + 512]
                trk2["pst_free"] = ("vs", v_op(lambda e, d=dst: e.tensor_copy(d, ps_t2[:, :])))
                yield
            trk2["ready"][bi] = (p.cnt["vs"], p.cnt["ss"])

        def drain(g):
            if g is None:
                return
            for _ in g:
                pass

        def chain_and_seq(c, heads0, gen):
            bi = c % 2

            def fill():
                if gen is not None:
                    try:
                        next(gen)
                    except StopIteration:
                        pass

            rdy_v, rdy_s = trk2["ready"][bi]
            kf = fbf2[bi][:, 0:1024]
            kbf = fbf2[bi][:, 1024:2048]
            p.W("sync", "vs", rdy_v)
            p.W("sync", "spk", trk["spill_kbf"])
            trk["spill_kbf"] = sy_dma(
                kbf_s.rearrange("h p nc n -> p h nc n")[:, heads0:heads0 + 8, c, :],
                kbf.rearrange("p (h n) -> p h n", h=8), sem="spk")
            t_wait("vs", rdy_v)
            for k4 in range(4):
                p.W("tensor", *trk2["chain_free"][k4])
            for h in range(8):
                hs = slice(h * CH, (h + 1) * CH)
                mm(raw_slot(h), kf[:, hs], kbf[:, hs], start=True, stop=True)
                mm(T_slot(h), kbf[:, hs], kf[:, hs], start=True, stop=True,
                   inc=(h == 7))
            kkt = p.cnt["ts"]
            trk2["fbf_free"][bi] = (kkt, trk["spill_kbf"])
            fill()
            v_wait("ts", kkt)
            for k4 in range(4):
                po = chain_ps[k4]
                v_op(lambda e, k4=k4, po=po: e.tensor_tensor(
                    Araw_all.ap()[:, k4 * 256:(k4 + 1) * 256].rearrange("p (b n) -> p b n", n=CH),
                    po.ap()[:, 0:256].rearrange("p (b n) -> p b n", n=CH),
                    trilm_b.ap().unsqueeze(1).broadcast_to((CH, 2, CH)),
                    ALU.mult), inc=False)
                v_op(lambda e, k4=k4, po=po: e.tensor_tensor(
                    AT_all.ap()[:, k4 * 256:(k4 + 1) * 256].rearrange("p (b n) -> p b n", n=CH),
                    po.ap()[:, 256:512].rearrange("p (b n) -> p b n", n=CH),
                    trium_b.ap().unsqueeze(1).broadcast_to((CH, 2, CH)),
                    ALU.mult))
            amasks = v_op(lambda e: e.tensor_scalar(nAT_all[:, :], AT_all[:, :], -1.0,
                                                    None, ALU.mult))
            for k4 in range(4):
                trk2["chain_free"][k4] = ("vs", amasks)
            fill()
            v_wait("vs", rdy_v)
            rv_dst = R_all[0].ap().rearrange("p (h n) -> p h n", n=256)
            v_op(lambda e, bi=bi: e.tensor_copy(
                rv_dst[:, :, 0:128],
                v_bf2[bi].ap().rearrange("p (h n) -> p h n", n=128)), inc=False)
            rinit = v_op(lambda e, bi=bi: e.tensor_copy(
                rv_dst[:, :, 128:256],
                t_bf2[bi].ap()[:, 0:1024].rearrange("p (h n) -> p h n", n=128)))
            trk2["vbf_free"][bi] = rinit
            rcur = 0
            for j in range(7):
                pj, pn = j % 2, (j + 1) % 2
                for w in range(2):
                    if j == 0:
                        t_wait("vs", amasks)
                        t_wait("vs", rinit)
                    for hh in range(4):
                        h = w * 4 + hh
                        hs = slice(h * CH, (h + 1) * CH)
                        po = chain_ps[hh]
                        p.W("tensor", *trk2["chain_free"][hh])
                        if j == 0:
                            praw_v, pT_v, facT = Araw_all[:, hs], AT_all[:, hs], nAT_all[:, hs]
                        else:
                            base = h * 256
                            praw_v = PWb[pj][:, base:base + 128]
                            pT_v = PWb[pj][:, base + 128:base + 256]
                            facT = pT_v
                        Rv = R_all[rcur][:, h * 256:(h + 1) * 256]
                        if j < 6:
                            mm(po[:, 0:128], pT_v, praw_v, start=True, stop=True)
                            mm(po[:, 128:256], praw_v, pT_v, start=True, stop=True)
                        mm(po[:, 256:512], ident_b[:, :], Rv, start=True, stop=False)
                        mm(po[:, 256:512], facT, Rv, start=False, stop=True,
                           inc=(hh == 3))
                    gmm = p.cnt["ts"]
                    fill()
                    v_wait("ts", gmm)
                    s_wait("ts", gmm)
                    for hh in range(4):
                        h = w * 4 + hh
                        po = chain_ps[hh]
                        dstP = PWb[pn][:, h * 256:(h + 1) * 256]
                        dstR = R_all[1 - rcur][:, h * 256:(h + 1) * 256]
                        if hh < 2:
                            if j < 6:
                                v_op(lambda e, d=dstP, po=po: e.tensor_copy(d, po[:, 0:256]),
                                     inc=False)
                            ev_ = v_op(lambda e, d=dstR, po=po: e.tensor_copy(d, po[:, 256:512]))
                            trk2["chain_free"][hh] = ("vs", ev_)
                        else:
                            if j < 6:
                                s_op(lambda e, d=dstP, po=po: e.copy(d, po[:, 0:256]),
                                     inc=False)
                            ev_ = s_op(lambda e, d=dstR, po=po: e.copy(d, po[:, 256:512]))
                            trk2["chain_free"][hh] = ("ss", ev_)
                rcur = 1 - rcur
            last_r_v = p.cnt["vs"]
            last_r_s = p.cnt["ss"]
            v_wait("ss", last_r_s)
            u_cp = v_op(lambda e, rc=rcur: e.tensor_copy(
                U_all.ap().rearrange("p (h n) -> p h n", n=128),
                R_all[rc].ap().rearrange("p (h n) -> p h n", n=256)[:, :, 0:128]))
            for g in range(2):
                for q4 in range(4):
                    h = g * 4 + q4
                    t_wait("vs", last_r_v)
                    t_wait("ss", last_r_s)
                    p.W("tensor", *trk2["pst_free"])
                    transpose(ps_t2[:, q4 * 128:(q4 + 1) * 128],
                              R_all[rcur][:, h * 256 + 128: h * 256 + 256])
                tpd = p.cnt["ts"]
                v_wait("ts", tpd)
                trk2["pst_free"] = ("vs", v_op(lambda e, g=g: e.tensor_copy(
                    Wf_all[:, g * 512:(g + 1) * 512], ps_t2[:, :])))
            wf_done = p.cnt["vs"]
            drain(gen)
            t_wait("vs", wf_done)
            t_wait("vs", seqw["s_upd"])
            t_wait("vs", trk["pp_evac"][0])
            t_wait("vs", trk["pp_evac"][1])
            for h in range(8):
                pb = pp[h // 4]
                mm(pb[:, (h % 4) * 128:(h % 4) * 128 + 128], Wf_all[:, h * CH:(h + 1) * CH],
                   S_bf[:, h * HD:(h + 1) * HD], start=True, stop=True, inc=(h == 7))
            psv_done = p.cnt["ts"]
            v_wait("ts", psv_done)
            v_wait("spv", trk["spill_vn"])
            v_op(lambda e: e.tensor_tensor(vn_buf[:, 0:512], U_all[:, 0:512],
                                           pp[0][:, :], ALU.subtract), inc=False)
            vn_done = v_op(lambda e: e.tensor_tensor(vn_buf[:, 512:1024], U_all[:, 512:1024],
                                                     pp[1][:, :], ALU.subtract))
            trk["pp_evac"][0] = vn_done
            trk["pp_evac"][1] = vn_done
            p.W("sync", "vs", vn_done)
            trk["spill_vn"] = sy_dma(
                vnew_s.rearrange("h p nc n -> p h nc n")[:, heads0:heads0 + 8, c, :],
                vn_buf.ap().rearrange("p (h n) -> p h n", h=8), sem="spv")
            t_wait("vs", vn_done)
            p.W("tensor", *trk2["chain_free"][2])
            p.W("tensor", *trk2["chain_free"][3])
            for h in range(8):
                prb = ps_r[h // 4]
                mm(prb[:, (h % 4) * 128:(h % 4) * 128 + 128],
                   t_bf2[bi][:, 1024 + h * HD: 1024 + (h + 1) * HD],
                   vn_buf[:, h * HD:(h + 1) * HD], start=True, stop=True, inc=(h == 7))
            trk2["tbf_free"][bi] = p.cnt["ts"]
            v_wait("ts", p.cnt["ts"])
            v_op(lambda e: e.tensor_tensor(S_f32[:, 0:512], S_f32[:, 0:512],
                                           ps_r[0][:, :], ALU.add), inc=False)
            sadd = v_op(lambda e: e.tensor_tensor(S_f32[:, 512:1024], S_f32[:, 512:1024],
                                                  ps_r[1][:, :], ALU.add))
            trk2["chain_free"][2] = ("vs", sadd)
            trk2["chain_free"][3] = ("vs", sadd)
            v_wait("vs", sadd)
            seqw["s_upd"] = v_op(lambda e: e.tensor_copy(S_bf[:, :], S_f32[:, :]))

        seqw = {"s_upd": 0}
        for hgrp in range(2):
            heads0 = hgrp * 8
            col0 = hgrp * 1024
            barrier()
            lw1 = gp_dma(wres.ap().rearrange("p (dc n) -> p dc n", n=2048)[:, :, 0:1024],
                         wk.rearrange("(dc p) n -> p dc n", p=128)[:, :, col0:col0 + 1024], sem="wld")
            lw2 = gp_dma(wres.ap().rearrange("p (dc n) -> p dc n", n=2048)[:, :, 1024:2048],
                         wv.rearrange("(dc p) n -> p dc n", p=128)[:, :, col0:col0 + 1024], sem="wld")
            lw3 = gp_dma(wb_res.ap().rearrange("p (dc n) -> p dc n", n=8),
                         wb.rearrange("(dc p) n -> p dc n", p=128)[:, :, hgrp * 8:hgrp * 8 + 8], sem="wld")
            v_op(lambda e: e.memset(S_f32[:, :], 0.0), inc=False)
            sz = v_op(lambda e: e.memset(S_bf[:, :], 0.0))
            seqw["s_upd"] = sz
            g0 = stage1(0, [("wld", lw3)])
            drain(g0)
            for c in range(nch):
                gen = stage1(c + 1, []) if c + 1 < nch else None
                chain_and_seq(c, heads0, gen)

    barrier()
    astack.close()


    ot_dbg = dram("ot_dbg", (HD, H * OWN), BF16, kind="ExternalOutput").ap() if dbg_phase_a else None
    bstack = None
    if "B" in phases:
        cur["t"], cur["v"], cur["s"] = "ts2", "vs2", "ss2"
        bstack = contextlib.ExitStack()
        bstack.__enter__()

        def bsbuf(name, shape, dtype):
            return bstack.enter_context(nc.sbuf_tensor(name, shape, dtype))

        def bpsum(name, shape, dtype=F32):
            return bstack.enter_context(nc.psum_tensor(name, shape, dtype))

        otstack = contextlib.ExitStack()
        otstack.__enter__()
        OT_acc = otstack.enter_context(nc.sbuf_tensor("OT_acc", [HD, H * OWN], BF16))
        qfb = [bsbuf(f"qfb{i}", [HD, NOC * CH], BF16) for i in range(2)]
        kbb = [bsbuf(f"kbb{i}", [HD, NCH * CH], BF16) for i in range(2)]
        vnb = [bsbuf(f"vnb{i}", [CH, NCH * HD], BF16) for i in range(2)]
        PTb = [bsbuf(f"PTb{i}", [128, 1024], BF16) for i in range(2)]
        rrow = bsbuf("rrow", [1, TOK_TILE], F32)
        rbc = bsbuf("rbc", [128, TOK_TILE], F32)
        ST_ps = [bpsum(f"ST_ps{i}", [128, 1024], F32) for i in range(2)]
        OT_ps = [bpsum("OT_ps0", [128, TOK_TILE], F32)]
        OT_ps.append(OT_ps[0])
        dn_ps = [bpsum("dn_ps0", [128, TOK_TILE], F32)]
        dn_ps.append(dn_ps[0])
        bc_ps2 = bpsum("bc_ps2", [128, TOK_TILE], F32)
        ones_sq = bsbuf("ones_sq", [128, 128], BF16)
        v_op(lambda e: e.memset(ones_sq[:, :], 1.0))

        barrier()
        hb_cons = [0, 0]
        st_free = [0, 0]
        pt_free = [0, 0]
        ot_evac = [0, 0]
        dn_evac = [0, 0]
        bcb_evac = 0
        ldh = [0, 0]

        def load_head(hh):
            sl = hh % 2
            p.W("sync", "ts2", hb_cons[sl])
            sy_dma(qfb[sl].ap()[:, 0:noc * CH],
                   qf_s[hh, :, 0:noc].rearrange("p nc n -> p (nc n)"), sem=f"hb{sl}")
            sy_dma(kbb[sl].ap()[:, 0:nch * CH],
                   kbf_s[hh, :, 0:nch].rearrange("p nc n -> p (nc n)"), sem=f"hb{sl}")
            ldh[sl] = sy_dma(vnb[sl].ap()[:, 0:nch * HD],
                             vnew_s[hh, :, 0:nch].rearrange("p nc n -> p (nc n)"), sem=f"hb{sl}")

        if dbg_zero_ot:
            v_op(lambda e: e.memset(OT_acc[:, :], 0.0))
        load_head(0)
        load_head(1)
        for h in range(H):
            sl = h % 2
            t_wait(f"hb{sl}", ldh[sl])
            for j in range(nqt):
                nkb = 8 * j + 8
                for kk in range(0, nkb, 2):
                    pb2 = (kk // 2) % 2
                    p.W("tensor", "ss2", st_free[pb2])
                    mm(ST_ps[pb2][:, 0:512], kbb[sl][:, kk * CH:(kk + 1) * CH],
                       qfb[sl][:, j * 512:(j + 1) * 512], start=True, stop=True)
                    smm = mm(ST_ps[pb2][:, 512:1024], kbb[sl][:, (kk + 1) * CH:(kk + 2) * CH],
                             qfb[sl][:, j * 512:(j + 1) * 512], start=True, stop=True, inc=True)
                    s_wait("ts2", smm)
                    p.W("scalar", "ts2", pt_free[pb2])
                    st_free[pb2] = s_op(lambda e, pb2=pb2, h=h: e.activation(
                        PTb[pb2][:, :], ST_ps[pb2][:, :], AF.Exp,
                        bias=Mneg[:, h:h + 1], scale=SCALE))
                    pt_ready_s = st_free
                    pt_ready_s = st_free[pb2]
                    if kk >= 8 * j:
                        m = kk - 8 * j
                        v_wait("ss2", pt_ready_s)
                        v_op(lambda e, pb2=pb2, m=m: e.tensor_tensor(
                            PTb[pb2][:, 0:512], PTb[pb2][:, 0:512],
                            cmask_b[:, m * 512:(m + 1) * 512], ALU.mult))
                        mdone = v_op(lambda e, pb2=pb2, m=m: e.tensor_tensor(
                            PTb[pb2][:, 512:1024], PTb[pb2][:, 512:1024],
                            cmask_b[:, (m + 1) * 512:(m + 2) * 512], ALU.mult))
                        t_wait("vs2", mdone)
                    else:
                        t_wait("ss2", pt_ready_s)
                    if kk == 0:
                        t_wait("vs2", ot_evac[j % 2])
                        t_wait("ss2", dn_evac[j % 2])
                    mm(OT_ps[j % 2][:, :], vnb[sl][:, kk * HD:(kk + 1) * HD],
                       PTb[pb2][:, 0:512], start=(kk == 0), stop=False)
                    mm(OT_ps[j % 2][:, :], vnb[sl][:, (kk + 1) * HD:(kk + 2) * HD],
                       PTb[pb2][:, 512:1024], start=False, stop=(kk + 2 == nkb))
                    mm(dn_ps[j % 2][0:1, :], ones_col[:, 0:1],
                       PTb[pb2][:, 0:512], start=(kk == 0), stop=False)
                    pvmm = mm(dn_ps[j % 2][0:1, :], ones_col[:, 0:1],
                              PTb[pb2][:, 512:1024], start=False, stop=(kk + 2 == nkb),
                              inc=True)
                    pt_free[pb2] = pvmm
                s_wait("ts2", pvmm)
                s_op(lambda e, j=j: e.activation(rrow[:, :], dn_ps[j % 2][0:1, :], AF.Ln))
                dn_evac[j % 2] = s_op(lambda e: e.activation(rrow[:, :], rrow[:, :],
                                                             AF.Exp, scale=-1.0))
                t_wait("ss2", dn_evac[j % 2])
                t_wait("vs2", bcb_evac)
                bmm = mm(bc_ps2[:, :], ones_row_f[0:1, :], rrow[0:1, :],
                         start=True, stop=True, inc=True)
                v_wait("ts2", bmm)
                bcb_evac = v_op(lambda e: e.tensor_copy(rbc[:, :], bc_ps2[:, :]))
                ot_evac[j % 2] = v_op(lambda e, h=h, j=j: e.tensor_tensor(
                    OT_acc[:, h * OWN + j * 512: h * OWN + (j + 1) * 512],
                    OT_ps[j % 2][:, :], rbc[:, :], ALU.mult))
            hb_cons[sl] = p.cnt["ts2"]
            if h + 2 < H:
                load_head(h + 2)
        if dbg_phase_a:
            p.W("sync", "vs2", p.cnt["vs2"])
            sy_dma(ot_dbg, OT_acc[:, :], sem="sph")
        barrier()
        bstack.close()

    if "C" in phases:
        cur["t"], cur["v"], cur["s"] = "ts2", "vs2", "ss2"
        cstack = contextlib.ExitStack()
        cstack.__enter__()

        def csbuf(name, shape, dtype):
            return cstack.enter_context(nc.sbuf_tensor(name, shape, dtype))

        def cpsum(name, shape, dtype=F32):
            return cstack.enter_context(nc.psum_tensor(name, shape, dtype))

        ld_wo = gp_dma(wres.ap().rearrange("p (ac n) -> p ac n", n=2048),
                       wo.rearrange("(ac p) n -> p ac n", p=128), sem="cwg")
        g_res = csbuf("g_res", [128, DC * NE], F32)
        iota16 = csbuf("iota16", [128, NE], F32)
        iotatok = csbuf("iotatok", [128, NE], U32)
        xrs = [csbuf(f"xrs{i}", [128, 512], F32) for i in range(2)]
        hpre = csbuf("hpre", [128, DC * 512], F32)
        hbf_s = csbuf("hbf_s", [128, 2 * 512], BF16)
        sqbf_s = csbuf("sqbf_s", [128, 2 * 512], BF16)
        r1 = csbuf("r1", [1, 512], F32)
        r2 = csbuf("r2", [1, 512], F32)
        mu2 = csbuf("mu2", [1, 512], F32)
        rb = csbuf("rb", [128, 512], F32)
        mb_ = csbuf("mb_", [128, 512], F32)
        gs = csbuf("gs", [128, NE], F32)
        mx8 = csbuf("mx8", [128, 8], F32)
        mi8 = csbuf("mi8", [128, 8], U32)
        i1f = csbuf("i1f", [128, 2], F32)
        wnorm = csbuf("wnorm", [128, 3], F32)
        msk1 = csbuf("msk1", [128, NE], F32)
        msk2 = csbuf("msk2", [128, NE], F32)
        Mbf = csbuf("Mbf", [128, NE], BF16)
        rloc = csbuf("rloc", [128, NE], F32)
        tilebase = csbuf("tilebase", [1, NE], F32)
        csum = csbuf("csum", [1, NE], F32)
        offs = csbuf("offs", [128, 4], F32)
        selt = csbuf("selt", [128, NE], F32)
        offu = csbuf("offu", [128, 2], U32)
        fill_u = csbuf("fill_u", [128, 64], U32)
        fill_f = csbuf("fill_f", [128, 64], F32)
        h1trow = csbuf("h1trow", [128, D], F32)

        Y_ps = [cpsum(f"Y_ps{i}", [128, 512], F32) for i in range(2)]
        S1_ps = cpsum("S1_ps", [1, 512], F32)
        S2_ps = cpsum("S2_ps", [1, 512], F32)
        BC_ps = cpsum("BC_ps", [128, 512], F32)
        G_ps = cpsum("G_ps", [128, NE], F32)
        TP_ps = cpsum("TP_ps", [128, 512], F32)

        cregs = {}

        def _alloc_bc(e):
            cregs["bc_idx"] = e.alloc_register("bc_idx")
            cregs["bc_tok"] = e.alloc_register("bc_tok")
            e.reg_mov(cregs["bc_idx"], NE * CAP - 1)
            return e.reg_mov(cregs["bc_tok"], OWN - 1)

        p.E("gpsimd", _alloc_bc)
        ld_g = sy_dma(g_res.ap().rearrange("p (dc n) -> p dc n", n=NE),
                      gwT.rearrange("(dc p) n -> p dc n", p=128), sem="cw")
        sy_dma(iota16[:, :], iota16_in, sem="cw")
        ld_i = sy_dma(iotatok[:, :], iotatok_in, sem="cw")
        t_wait("cw", ld_i)
        t_wait("cwg", ld_wo)
        v_wait("cw", ld_i)
        s_wait("cw", ld_i)
        v_op(lambda e: e.memset(fill_u[:, :], 100000), inc=False)
        fz = v_op(lambda e: e.memset(fill_f[:, :], 0.0))
        p.W("sync", "vs2", fz)
        pre1 = sy_dma(idx_s.rearrange("(p b) one -> p (b one)", p=128),
                      fill_u[:, 0:NE * CAP // 128], sem="pre")
        pre2 = sy_dma(wg_s.rearrange("(p b) one -> p (b one)", p=128),
                      fill_f[:, 0:NE * CAP // 128], sem="pre")

        zb = v_op(lambda e: e.memset(tilebase[:, :], 0.0))

        y_evac = [0, 0]
        stat_guard = [0, 0]
        stats_state = {"smm": 0}
        h1_spill = 0
        gmm_last = 0
        gsig_last = 0
        stats_read = 0
        bcc_evac = 0
        scat_cnt = 0
        tp_evac = 0
        h1t_spill = 0
        xr_cons = [0, 0]
        for t in range(NQT):
            p.W("vector", "h1f0", h1_spill)
            pend = None
            for dco in range(DC):
                xsl = dco % 2
                p.W("sync", "vs2", xr_cons[xsl])
                ldx2 = sy_dma(xrs[xsl][:, :],
                              xq.rearrange("(dc p) n -> p dc n", p=128)[:, dco,
                                           t * 512:(t + 1) * 512], sem=f"xr{xsl}")
                yp = Y_ps[dco % 2]
                t_wait("vs2", y_evac[dco % 2])
                for ac in range(H):
                    mm(yp[:, :], wres[:, ac * 2048 + dco * 128: ac * 2048 + dco * 128 + 128],
                       OT_acc[:, ac * OWN + t * 512: ac * OWN + (t + 1) * 512],
                       start=(ac == 0), stop=(ac == H - 1), inc=(ac == H - 1))
                wmm = p.cnt["ts2"]
                if pend is not None:
                    pend()
                v_wait("ts2", wmm)
                v_wait(f"xr{xsl}", ldx2)
                y_evac[dco % 2] = v_op(lambda e, dco=dco, yp=yp, xsl=xsl: e.tensor_tensor(
                    hpre[:, dco * 512:(dco + 1) * 512], yp[:, :],
                    xrs[xsl][:, :], ALU.add))
                xr_cons[xsl] = y_evac[dco % 2]
                s_wait("vs2", y_evac[dco % 2])
                s_wait("ts2", stat_guard[dco % 2])
                s_op(lambda e, dco=dco: e.copy(
                    hbf_s[:, (dco % 2) * 512:(dco % 2) * 512 + 512],
                    hpre[:, dco * 512:(dco + 1) * 512]), inc=False)
                sq_d = s_op(lambda e, dco=dco: e.activation(
                    sqbf_s[:, (dco % 2) * 512:(dco % 2) * 512 + 512],
                    hpre[:, dco * 512:(dco + 1) * 512], AF.Square))

                def pend(dco=dco, sq_d=sq_d):
                    par = dco % 2
                    if dco == 0:
                        t_wait("vs2", stats_read)
                    t_wait("ss2", sq_d)
                    mm(S1_ps[0:1, :], ones_col[:, 0:1],
                       hbf_s[:, par * 512:par * 512 + 512],
                       start=(dco == 0), stop=(dco == DC - 1))
                    stats_state["smm"] = mm(
                        S2_ps[0:1, :], ones_col[:, 0:1],
                        sqbf_s[:, par * 512:par * 512 + 512],
                        start=(dco == 0), stop=(dco == DC - 1), inc=True)
                    stat_guard[par] = p.cnt["ts2"]
            pend()
            smm = stats_state["smm"]
            v_wait("ts2", smm)
            v_op(lambda e: e.tensor_scalar(r1[:, :], S1_ps[0:1, :], 1.0 / D, None, ALU.mult))
            stats_read = v_op(lambda e: e.tensor_scalar(r2[:, :], S2_ps[0:1, :], 1.0 / D, None, ALU.mult))
            s_wait("vs2", stats_read)
            sq_mu = s_op(lambda e: e.activation(mu2[:, :], r1[:, :], AF.Square))
            v_wait("ss2", sq_mu)
            vv = v_op(lambda e: e.tensor_tensor(r2[:, :], r2[:, :], mu2[:, :], ALU.subtract))
            s_wait("vs2", vv)
            s_op(lambda e: e.activation(r2[:, :], r2[:, :], AF.Ln, bias=epsln_b[:, :]))
            rstd_d = s_op(lambda e: e.activation(r2[:, :], r2[:, :], AF.Exp, scale=-0.5))
            v_wait("ss2", rstd_d)
            murs_d = v_op(lambda e: e.tensor_tensor(r1[:, :], r1[:, :], r2[:, :], ALU.mult))
            t_wait("vs2", murs_d)
            t_wait("vs2", bcc_evac)
            bmm = mm(BC_ps[:, :], ones_row_f[0:1, :], r2[0:1, :], start=True, stop=True, inc=True)
            v_wait("ts2", bmm)
            e1 = v_op(lambda e: e.tensor_copy(rb[:, :], BC_ps[:, :]))
            t_wait("vs2", e1)
            bmm = mm(BC_ps[:, :], ones_row_f[0:1, :], r1[0:1, :], start=True, stop=True, inc=True)
            v_wait("ts2", bmm)
            bcc_evac = v_op(lambda e: e.tensor_copy(mb_[:, :], BC_ps[:, :]))
            p.W("vector", "h1f0", h1_spill)
            v_wait("ts2", gmm_last)
            for dco in range(DC):
                v_op(lambda e, dco=dco: e.tensor_tensor(
                    hpre[:, dco * 512:(dco + 1) * 512], hpre[:, dco * 512:(dco + 1) * 512],
                    rb[:, :], ALU.mult))
                nrm = v_op(lambda e, dco=dco: e.tensor_tensor(
                    hpre[:, dco * 512:(dco + 1) * 512], hpre[:, dco * 512:(dco + 1) * 512],
                    mb_[:, :], ALU.subtract))
            p.W("sync", "vs2", nrm)
            h1_spill = sy_dma(h1_s.rearrange("dc p n -> p dc n")[:, :, t * 512:(t + 1) * 512],
                              hpre.ap().rearrange("p (dc n) -> p dc n", n=512), sem="h1f0")
            for sub in range(4):
                for dco in range(DC):
                    t_wait("vs2", nrm)
                    if dco % 4 == 0:
                        t_wait("vs2", tp_evac)
                    transpose(TP_ps[:, (dco % 4) * 128:(dco % 4) * 128 + 128],
                              hpre[:, dco * 512 + sub * 128: dco * 512 + sub * 128 + 128],
                              ident=ident_f)
                    if dco % 4 == 3:
                        tpm = p.cnt["ts2"]
                        v_wait("ts2", tpm)
                        p.W("vector", "h1t", h1t_spill)
                        p.W("vector", "h1tb", p.cnt["h1tb"])
                        tp_evac = v_op(lambda e, dco=dco, sub=sub: e.tensor_copy(
                            h1trow[:, (dco - 3) * 128:(dco + 1) * 128], TP_ps[:, :]))
                p.W("sync", "vs2", tp_evac)
                sy_dma(h1tokf_s[t * 512 + sub * 128: t * 512 + (sub + 1) * 128, :],
                       h1trow[:, :], sem="h1t")
                h1t_spill = p.cnt["h1t"]
                p.W("gpsimd", "vs2", tp_evac)
                gp_dma(h1tok_s[t * 512 + sub * 128: t * 512 + (sub + 1) * 128, :],
                       h1trow[:, :], sem="h1tb")
            for sub in range(4):
                t_wait("vs2", nrm)
                t_wait("ss2", gsig_last)
                t_wait("vs2", trk.get("gps_read", 0))
                for dco in range(DC):
                    mm(G_ps[:, :], hpre[:, dco * 512 + sub * 128: dco * 512 + sub * 128 + 128],
                       g_res[:, dco * NE:(dco + 1) * NE],
                       start=(dco == 0), stop=(dco == DC - 1), inc=(dco == DC - 1))
                gmm_last = p.cnt["ts2"]
                s_wait("ts2", gmm_last)
                gsig_last = s_op(lambda e: e.activation(gs[:, :], G_ps[:, :], AF.Sigmoid))
                v_wait("ss2", gsig_last)
                p.W("vector", "scx", scat_cnt)
                v_op(lambda e: e.max(mx8[:, :], gs[:, :]))
                v_op(lambda e: e.max_index(mi8[:, :], mx8[:, :], gs[:, :]))
                v_op(lambda e: e.tensor_copy(i1f[:, 0:2], mi8[:, 0:2]))
                sden = v_op(lambda e: e.tensor_tensor(wnorm[:, 0:1], mx8[:, 0:1], mx8[:, 1:2], ALU.add))
                s_wait("vs2", sden)
                s_op(lambda e: e.activation(wnorm[:, 1:2], wnorm[:, 0:1], AF.Ln))
                srcp = s_op(lambda e: e.activation(wnorm[:, 1:2], wnorm[:, 1:2], AF.Exp, scale=-1.0))
                v_wait("ss2", srcp)
                v_op(lambda e: e.tensor_tensor(wnorm[:, 0:1], mx8[:, 0:1], wnorm[:, 1:2], ALU.mult))
                v_op(lambda e: e.tensor_tensor(wnorm[:, 2:3], mx8[:, 1:2], wnorm[:, 1:2], ALU.mult))
                v_op(lambda e: e.tensor_scalar(msk1[:, :], iota16[:, :], i1f[:, 0:1], None, ALU.is_equal))
                mm2 = v_op(lambda e: e.tensor_scalar(msk2[:, :], iota16[:, :], i1f[:, 1:2], None, ALU.is_equal))
                v_wait("vs2", mm2)
                mjd = v_op(lambda e: e.tensor_tensor(rloc[:, :], msk1[:, :], msk2[:, :], ALU.add))
                v_wait("vs2", mjd)
                mbd = v_op(lambda e: e.tensor_copy(Mbf[:, :], rloc[:, :]))
                t_wait("vs2", mbd)
                t_wait("vs2", p.cnt["vs2"])
                rmm = mm(G_ps[:, :], trium_b[:, :], Mbf[:, :], start=True, stop=True, inc=True)
                bbm = mm(BC_ps[:, 0:NE], ones_row_f[0:1, :], tilebase[0:1, :],
                         start=True, stop=True, inc=True)
                csm = mm(TP_ps[0:1, 0:NE], ones_col[:, 0:1], Mbf[:, :], start=True, stop=True, inc=True)
                v_wait("ts2", csm)
                v_op(lambda e: e.tensor_copy(rloc[:, :], G_ps[:, :]), inc=False)
                rgd = v_op(lambda e: e.tensor_tensor(rloc[:, :], rloc[:, :], BC_ps[:, 0:NE], ALU.add))
                trk["gps_read"] = rgd
                v_op(lambda e: e.tensor_tensor(csum[:, :], TP_ps[0:1, 0:NE], tilebase[:, :], ALU.add))
                tbd = v_op(lambda e: e.tensor_copy(tilebase[:, :], csum[:, :]))
                for ch in range(2):
                    mskc = msk1 if ch == 0 else msk2
                    v_op(lambda e, mskc=mskc: e.tensor_tensor(selt[:, :], rloc[:, :], mskc[:, :], ALU.mult))
                    rs = v_op(lambda e: e.tensor_reduce(
                        offs.ap()[:, 2:3], selt.ap()[:, :], AX.X, ALU.add))
                    v_wait("vs2", rs)
                    v_op(lambda e: e.tensor_scalar(offs[:, 3:4], offs[:, 2:3], float(CAP), None, ALU.is_ge))
                    v_op(lambda e, ch=ch: e.scalar_tensor_tensor(
                        offs[:, ch:ch + 1], offs[:, 3:4], 100000.0, offs[:, 2:3],
                        ALU.mult, ALU.add))
                    v_op(lambda e, ch=ch: e.scalar_tensor_tensor(
                        offs[:, ch:ch + 1], i1f[:, ch:ch + 1], float(CAP), offs[:, ch:ch + 1],
                        ALU.mult, ALU.add))
                    cvt = v_op(lambda e, ch=ch: e.tensor_copy(offu[:, ch:ch + 1], offs[:, ch:ch + 1]))
                p.W("gpsimd", "vs2", cvt)
                p.W("gpsimd", "pre", pre2)
                p.W("gpsimd", "scx", scat_cnt)
                tokcol = iotatok[:, t * 4 + sub: t * 4 + sub + 1]
                gp_dma_scatter = None
                for ch in range(2):
                    wcol = wnorm[:, 0:1] if ch == 0 else wnorm[:, 2:3]
                    p.inc("scx", 16)
                    p.E("gpsimd", lambda e, ch=ch, tokcol=tokcol: e.indirect_dma_start(
                        idx_s, IndirectOffsetOnAxis(offu[:, ch:ch + 1], 0),
                        tokcol, None, bounds_check=cregs["bc_idx"], oob_is_err=False
                    ).then_inc(L["scx"], 16))
                    p.inc("scx", 16)
                    p.E("gpsimd", lambda e, ch=ch, wcol=wcol: e.indirect_dma_start(
                        wg_s, IndirectOffsetOnAxis(offu[:, ch:ch + 1], 0),
                        wcol, None, bounds_check=cregs["bc_idx"], oob_is_err=False
                    ).then_inc(L["scx"], 16))
                scat_cnt = p.cnt["scx"]
        barrier()
        cstack.close()
        if "B" in phases:
            otstack.close()
        wstack.close()

        c2stack = contextlib.ExitStack()
        c2stack.__enter__()

        def c2sbuf(name, shape, dtype):
            return c2stack.enter_context(nc.sbuf_tensor(name, shape, dtype))

        def c2psum(name, shape, dtype=F32):
            return c2stack.enter_context(nc.psum_tensor(name, shape, dtype))

        h1bf = c2sbuf("h1bf", [128, DC * 1024], BF16)
        yT = c2sbuf("yT", [128, DC * 1024], F32)
        sa_bf = c2sbuf("sa_bf", [128, IC * 1024], BF16)
        hbuf = c2sbuf("hbuf", [128, IC * 1024], BF16)
        wseg = [c2sbuf(f"wseg{i}", [128, DC * 512], BF16) for i in range(2)]
        ytrow = c2sbuf("ytrow", [128, D], F32)
        M_ps = [c2psum(f"M_ps{i}", [128, 1024], F32) for i in range(2)]
        TPC_ps = c2psum("TPC_ps", [128, 512], F32)

        I_SEGS = [(0, 4), (4, 4), (8, 3)]

        wseg_cons = [0, 0]
        ldw_cnt = [0, 0]
        mps_free = [("ss2", 0), ("ss2", 0)]
        sa_done = {}
        h_done = {}
        h1b_cons = 0
        ytp_evac = 0
        yt_spill = 0

        def wload_c2(slot, cols, src_ap):
            qeng = "sync" if slot == 0 else "gpsimd"
            p.W(qeng, "ts2", wseg_cons[slot])
            fn = sy_dma if slot == 0 else gp_dma
            return fn(wseg[slot][:, 0:cols], src_ap, sem=f"ws{slot}")

        for r in range(2):
            p.W("gpsimd", "ts2", h1b_cons)
            ldh1 = gp_dma(h1bf.ap().rearrange("p (dc n) -> p dc n", n=1024),
                          h1_s.rearrange("dc p n -> p dc n")[:, :, r * 1024:(r + 1) * 1024],
                          sem="h1b")
            for kk, kind in enumerate(("s0", "s1")):
                for si, (ic0, icn) in enumerate(I_SEGS):
                    slot = si % 2
                    cols = 16 * icn * 128
                    ldw_cnt[slot] = wload_c2(
                        slot, cols,
                        sw1p[:, kk * W13_COLS + OFF1[si]: kk * W13_COLS + OFF1[si] + cols])
                    for ici in range(icn):
                        ic = ic0 + ici
                        mp = M_ps[ic % 2]
                        t_wait(f"ws{slot}", ldw_cnt[slot])
                        if kind == "s0" and ic == 0:
                            t_wait("h1b", ldh1)
                        p.W("tensor", *mps_free[ic % 2])
                        for dc in range(DC):
                            for tt in range(2):
                                mm(mp[:, tt * 512:(tt + 1) * 512],
                                   wseg[slot][:, (dc * icn + ici) * 128: (dc * icn + ici) * 128 + 128],
                                   h1bf[:, dc * 1024 + tt * 512: dc * 1024 + tt * 512 + 512],
                                   start=(dc == 0), stop=(dc == DC - 1),
                                   inc=(dc == DC - 1 and tt == 1))
                        am = p.cnt["ts2"]
                        s_wait("ts2", am)
                        sa_done[ic] = s_op(lambda e, ic=ic, mp=mp: e.activation(
                            sa_bf[:, ic * 1024:(ic + 1) * 1024], mp[:, :], AF.Silu))
                        mps_free[ic % 2] = ("ss2", sa_done[ic])
                    wseg_cons[slot] = p.cnt["ts2"]
                for si, (ic0, icn) in enumerate(I_SEGS):
                    slot = si % 2
                    cols = 16 * icn * 128
                    ldw_cnt[slot] = wload_c2(
                        slot, cols,
                        sw3p[:, kk * W13_COLS + OFF1[si]: kk * W13_COLS + OFF1[si] + cols])
                    for ici in range(icn):
                        ic = ic0 + ici
                        mp = M_ps[ic % 2]
                        t_wait(f"ws{slot}", ldw_cnt[slot])
                        p.W("tensor", *mps_free[ic % 2])
                        for dc in range(DC):
                            for tt in range(2):
                                mm(mp[:, tt * 512:(tt + 1) * 512],
                                   wseg[slot][:, (dc * icn + ici) * 128: (dc * icn + ici) * 128 + 128],
                                   h1bf[:, dc * 1024 + tt * 512: dc * 1024 + tt * 512 + 512],
                                   start=(dc == 0), stop=(dc == DC - 1),
                                   inc=(dc == DC - 1 and tt == 1))
                        bm = p.cnt["ts2"]
                        v_wait("ts2", bm)
                        v_wait("ss2", sa_done[ic])
                        h_done[ic] = v_op(lambda e, ic=ic, mp=mp: e.tensor_tensor(
                            hbuf[:, ic * 1024:(ic + 1) * 1024],
                            sa_bf[:, ic * 1024:(ic + 1) * 1024], mp[:, :], ALU.mult))
                        mps_free[ic % 2] = ("vs2", h_done[ic])
                    wseg_cons[slot] = p.cnt["ts2"]
                h_all = max(h_done.values())
                D_SEGS = [(0, 4), (4, 4), (8, 4), (12, 4)]
                for si, (dc0, dcn) in enumerate(D_SEGS):
                    slot = si % 2
                    cols = 11 * dcn * 128
                    ldw_cnt[slot] = wload_c2(
                        slot, cols,
                        sw2p[:, kk * W13_COLS + OFF2[si]: kk * W13_COLS + OFF2[si] + cols])
                    for dci in range(dcn):
                        dco = dc0 + dci
                        mp = M_ps[dco % 2]
                        t_wait(f"ws{slot}", ldw_cnt[slot])
                        p.W("tensor", *mps_free[dco % 2])
                        t_wait("vs2", h_all)
                        for ic in range(IC):
                            for tt in range(2):
                                mm(mp[:, tt * 512:(tt + 1) * 512],
                                   wseg[slot][:, (ic * dcn + dci) * 128: (ic * dcn + dci) * 128 + 128],
                                   hbuf[:, ic * 1024 + tt * 512: ic * 1024 + tt * 512 + 512],
                                   start=(ic == 0), stop=(ic == IC - 1),
                                   inc=(ic == IC - 1 and tt == 1))
                        ym = p.cnt["ts2"]
                        v_wait("ts2", ym)
                        if kind == "s0":
                            yv = v_op(lambda e, dco=dco, mp=mp: e.tensor_copy(
                                yT[:, dco * 1024:(dco + 1) * 1024], mp[:, :]))
                        else:
                            yv = v_op(lambda e, dco=dco, mp=mp: e.tensor_tensor(
                                yT[:, dco * 1024:(dco + 1) * 1024],
                                yT[:, dco * 1024:(dco + 1) * 1024], mp[:, :], ALU.add))
                        mps_free[dco % 2] = ("vs2", yv)
                    wseg_cons[slot] = p.cnt["ts2"]
            h1b_cons = p.cnt["ts2"]
            ysh_done = p.cnt["vs2"]
            for tt in range(2):
                for sub in range(4):
                    for dco in range(DC):
                        t_wait("vs2", ysh_done)
                        if dco % 4 == 0:
                            t_wait("vs2", ytp_evac)
                        transpose(TPC_ps[:, (dco % 4) * 128:(dco % 4) * 128 + 128],
                                  yT[:, dco * 1024 + tt * 512 + sub * 128:
                                     dco * 1024 + tt * 512 + sub * 128 + 128],
                                  ident=ident_f)
                        if dco % 4 == 3:
                            tpm = p.cnt["ts2"]
                            v_wait("ts2", tpm)
                            p.W("vector", "ytw", yt_spill)
                            ytp_evac = v_op(lambda e, dco=dco: e.tensor_copy(
                                ytrow[:, (dco - 3) * 128:(dco + 1) * 128], TPC_ps[:, :]))
                    p.W("sync", "vs2", ytp_evac)
                    row0 = r * 1024 + tt * 512 + sub * 128
                    yt_spill = sy_dma(y_tok_s[row0:row0 + 128, :], ytrow[:, :], sem="ytw")
        barrier()
        c2stack.close()

        c3stack = contextlib.ExitStack()
        c3stack.__enter__()

        def c3sbuf(name, shape, dtype):
            return c3stack.enter_context(nc.sbuf_tensor(name, shape, dtype))

        def c3psum(name, shape, dtype=F32):
            return c3stack.enter_context(nc.psum_tensor(name, shape, dtype))

        NSUB = CAP // 128
        gidx = c3sbuf("gidx", [128, NSUB], U32)
        wgrow = c3sbuf("wgrow", [1, CAP], F32)
        xe_tok = [c3sbuf(f"xe_tok{i}", [128, D], BF16) for i in range(2)]
        xef = c3sbuf("xef", [128, DC * CAP], BF16)
        sa2 = c3sbuf("sa2", [128, IC * CAP], BF16)
        h2 = c3sbuf("h2", [128, IC * CAP], BF16)
        wseg2 = [c3sbuf(f"wseg2_{i}", [128, DC * 512], BF16) for i in range(3)]
        outf = c3sbuf("outf", [128, DC * CAP], F32)
        strow = c3sbuf("strow", [128, D], F32)
        ME_ps = [c3psum(f"ME_ps{i}", [128, 512], F32) for i in range(2)]
        WG_ps = c3psum("WG_ps", [128, CAP], F32)
        PTX_ps = c3psum("PTX_ps", [128, 512], BF16)
        TPO_ps = c3psum("TPO_ps", [128, 512], F32)

        wseg2_cons = [0, 0, 0]
        ldw2 = [("ws0", 0)] * 3
        segctr = [0]

        def wload_c3(cols, src_ap):
            s = segctr[0]
            segctr[0] += 1
            slot = s % 3
            qeng = "sync" if s % 2 == 0 else "gpsimd"
            sem = "ws0" if s % 2 == 0 else "ws1"
            p.W(qeng, "ts2", wseg2_cons[slot])
            fn = sy_dma if s % 2 == 0 else gp_dma
            ldw2[slot] = (sem, fn(wseg2[slot][:, 0:cols], src_ap, sem=sem))
            return slot
        me_free = [("ss2", 0), ("ss2", 0)]
        ptx_free = ("vs2", 0)
        tpo_free = ("vs2", 0)
        sca_cnt = 0
        gi_cons = 0
        xe_cons = [0, 0]
        wg_read = 0
        st_cons = 0

        p.W("sync", "scx", p.cnt["scx"])
        p.W("sync", "ytw", p.cnt["ytw"])
        p.W("gpsimd", "scx", p.cnt["scx"])
        p.W("gpsimd", "h1tb", p.cnt["h1tb"])
        for ei in range(nex):
            p.W("sync", "gat", p.cnt["gat"])
            p.W("sync", "sca", p.cnt["sca"])
            for sub in range(NSUB):
                sy_dma(gidx[:, sub:sub + 1],
                       idx_s[ei * CAP + sub * 128: ei * CAP + (sub + 1) * 128, :],
                       sem="gld")
            p.W("sync", "ts2", wg_read)
            wld_ = sy_dma(wgrow[:, :],
                          wg_s[ei * CAP:(ei + 1) * CAP, 0:1].squeeze(1).unsqueeze(0),
                          sem="gld")
            for sub in range(NSUB):
                xp = sub % 2
                p.W("gpsimd", "gld", wld_)
                p.W("gpsimd", "ts2", xe_cons[xp])
                gat = p.inc("gat", 16)
                p.E("gpsimd", lambda e, sub=sub, xp=xp: e.indirect_dma_start(
                    xe_tok[xp][:, :], None, h1tok_s,
                    IndirectOffsetOnAxis(gidx[:, sub:sub + 1], 0),
                    bounds_check=cregs["bc_tok"], oob_is_err=False).then_inc(L["gat"], 16))
                for dco in range(DC):
                    t_wait("gat", gat)
                    if dco % 4 == 0:
                        p.W("tensor", *ptx_free)
                    transpose(PTX_ps[:, (dco % 4) * 128:(dco % 4) * 128 + 128],
                              xe_tok[xp][:, dco * 128:(dco + 1) * 128])
                    if dco % 4 == 3:
                        tpm = p.cnt["ts2"]
                        v_wait("ts2", tpm)
                        ptx_free = ("vs2", v_op(lambda e, dco=dco, sub=sub: e.tensor_copy(
                            xef.ap().rearrange("p (dc s) -> p dc s", s=CAP)[
                                :, dco - 3:dco + 1, sub * 128:(sub + 1) * 128],
                            PTX_ps.ap().rearrange("p (b n) -> p b n", n=128))))
                xe_cons[xp] = p.cnt["ts2"]
            xef_ready = p.cnt["vs2"]
            gi_cons = p.cnt["ts2"]
            t_wait("gld", wld_)
            t_wait("vs2", wg_read_v if ei > 0 else 0)
            mm(WG_ps[:, 0:CAP], ones_row_f[0:1, :], wgrow[0:1, 0:CAP], start=True, stop=True,
               inc=True)
            wgb = p.cnt["ts2"]
            wg_read = wgb
            for si, (ic0, icn) in enumerate(I_SEGS):
                cols = 16 * icn * 128
                slot = wload_c3(cols, w1p[ei][:, OFF1[si]:OFF1[si] + cols])
                for ici in range(icn):
                    ic = ic0 + ici
                    mp = ME_ps[ic % 2]
                    t_wait(*ldw2[slot])
                    v_wait2 = xef_ready
                    t_wait("vs2", v_wait2)
                    p.W("tensor", *me_free[ic % 2])
                    for dc in range(DC):
                        mm(mp[:, 0:CAP], wseg2[slot][:, (dc * icn + ici) * 128: (dc * icn + ici) * 128 + 128],
                           xef[:, dc * CAP: dc * CAP + CAP],
                           start=(dc == 0), stop=(dc == DC - 1), inc=(dc == DC - 1))
                    am = p.cnt["ts2"]
                    s_wait("ts2", am)
                    sa_d = s_op(lambda e, ic=ic, mp=mp: e.activation(
                        sa2[:, ic * CAP:(ic + 1) * CAP], mp[:, 0:CAP], AF.Silu))
                    me_free[ic % 2] = ("ss2", sa_d)
                wseg2_cons[slot] = p.cnt["ts2"]
            sa_all_s = p.cnt["ss2"]
            h_last = 0
            for si, (ic0, icn) in enumerate(I_SEGS):
                cols = 16 * icn * 128
                slot = wload_c3(cols, w3p[ei][:, OFF1[si]:OFF1[si] + cols])
                for ici in range(icn):
                    ic = ic0 + ici
                    mp = ME_ps[ic % 2]
                    t_wait(*ldw2[slot])
                    p.W("tensor", *me_free[ic % 2])
                    for dc in range(DC):
                        mm(mp[:, 0:CAP], wseg2[slot][:, (dc * icn + ici) * 128: (dc * icn + ici) * 128 + 128],
                           xef[:, dc * CAP: dc * CAP + CAP],
                           start=(dc == 0), stop=(dc == DC - 1), inc=(dc == DC - 1))
                    bm = p.cnt["ts2"]
                    v_wait("ts2", bm)
                    v_wait("ss2", sa_all_s)
                    v_op(lambda e, ic=ic, mp=mp: e.tensor_tensor(
                        h2[:, ic * CAP:(ic + 1) * CAP],
                        sa2[:, ic * CAP:(ic + 1) * CAP], mp[:, 0:CAP], ALU.mult))
                    hd = v_op(lambda e, ic=ic: e.tensor_tensor(
                        h2[:, ic * CAP:(ic + 1) * CAP],
                        h2[:, ic * CAP:(ic + 1) * CAP], WG_ps[:, 0:CAP], ALU.mult))
                    h_last = hd
                    me_free[ic % 2] = ("vs2", hd)
                wseg2_cons[slot] = p.cnt["ts2"]
            wg_read_v = h_last
            D_SEGS = [(0, 4), (4, 4), (8, 4), (12, 4)]
            of_last = 0
            for si, (dc0, dcn) in enumerate(D_SEGS):
                cols = 11 * dcn * 128
                slot = wload_c3(cols, w2p[ei][:, OFF2[si]:OFF2[si] + cols])
                for dci in range(dcn):
                    dco = dc0 + dci
                    mp = ME_ps[dco % 2]
                    t_wait(*ldw2[slot])
                    p.W("tensor", *me_free[dco % 2])
                    t_wait("vs2", h_last)
                    for ic in range(IC):
                        mm(mp[:, 0:CAP], wseg2[slot][:, (ic * dcn + dci) * 128: (ic * dcn + dci) * 128 + 128],
                           h2[:, ic * CAP: ic * CAP + CAP],
                           start=(ic == 0), stop=(ic == IC - 1), inc=(ic == IC - 1))
                    ym = p.cnt["ts2"]
                    v_wait("ts2", ym)
                    ofd = v_op(lambda e, dco=dco, mp=mp: e.tensor_copy(
                        outf[:, dco * CAP: dco * CAP + CAP], mp[:, 0:CAP]))
                    of_last = ofd
                    me_free[dco % 2] = ("vs2", ofd)
                wseg2_cons[slot] = p.cnt["ts2"]
            for sub in range(NSUB):
                for dco in range(DC):
                    t_wait("vs2", of_last)
                    if dco % 4 == 0:
                        p.W("tensor", *tpo_free)
                    transpose(TPO_ps[:, (dco % 4) * 128:(dco % 4) * 128 + 128],
                              outf[:, dco * CAP + sub * 128: dco * CAP + sub * 128 + 128],
                              ident=ident_f)
                    if dco % 4 == 3:
                        tpm = p.cnt["ts2"]
                        v_wait("ts2", tpm)
                        p.W("vector", "sca", sca_cnt)
                        tpo_free = ("vs2", v_op(lambda e, dco=dco: e.tensor_copy(
                            strow[:, (dco - 3) * 128:(dco + 1) * 128], TPO_ps[:, :])))
                p.W("gpsimd", "vs2", tpo_free[1])
                p.W("gpsimd", "sca", sca_cnt)
                p.inc("sca", 16)
                p.E("gpsimd", lambda e, sub=sub: e.indirect_dma_start(
                    y_tok_s, IndirectOffsetOnAxis(gidx[:, sub:sub + 1], 0),
                    strow[:, :], None, bounds_check=cregs["bc_tok"], oob_is_err=False,
                    compute_op=ALU.add).then_inc(L["sca"], 16))
                sca_cnt = p.cnt["sca"]
                st_cons = sca_cnt
        barrier()
        c3stack.close()

        c4stack = contextlib.ExitStack()
        c4stack.__enter__()
        yrow = [c4stack.enter_context(nc.sbuf_tensor(f"yrow{i}", [128, D], F32)) for i in range(2)]
        hrow = [c4stack.enter_context(nc.sbuf_tensor(f"hrow{i}", [128, D], F32)) for i in range(2)]
        orow = c4stack.enter_context(nc.sbuf_tensor("orow", [128, D], F32))
        sqrow = c4stack.enter_context(nc.sbuf_tensor("sqrow", [128, D], F32))
        strow2 = c4stack.enter_context(nc.sbuf_tensor("strow2", [128, 4], F32))
        epsc = c4stack.enter_context(nc.sbuf_tensor("epsc", [128, 1], F32))
        ld_e = sy_dma(epsc[:, :], epslncol_in, sem="cw")
        s_wait("cw", ld_e)
        yr_cons = [0, 0]
        out_cnt = 0
        sq_c = 0
        for tile in range(OWN // 128):
            sl = tile % 2
            p.W("sync", "vs2", yr_cons[sl])
            l1 = sy_dma(yrow[sl][:, :], y_tok_s[tile * 128:(tile + 1) * 128, :], sem=f"h1f{sl}")
            l2 = sy_dma(hrow[sl][:, :], h1tokf_s[tile * 128:(tile + 1) * 128, :], sem=f"h1f{sl}")
            v_wait(f"h1f{sl}", l2)
            radd = v_op(lambda e, sl=sl: e.tensor_tensor(yrow[sl][:, :], yrow[sl][:, :],
                                                         hrow[sl][:, :], ALU.add))
            yr_cons[sl] = radd
            v_op(lambda e, sl=sl: e.tensor_reduce(
                strow2.ap()[:, 0:1], yrow[sl].ap(), AX.X, ALU.add), inc=False)
            s_wait("vs2", radd)
            s_wait("ss2", sq_c)
            sq_c = s_op(lambda e, sl=sl: e.activation(sqrow[:, :], yrow[sl][:, :], AF.Square))
            v_wait("ss2", sq_c)
            rd2 = v_op(lambda e: e.tensor_reduce(
                strow2.ap()[:, 1:2], sqrow.ap(), AX.X, ALU.add))
            v_wait("vs2", rd2)
            v_op(lambda e: e.tensor_scalar(strow2[:, 0:1], strow2[:, 0:1], 1.0 / D, None, ALU.mult), inc=False)
            m2d = v_op(lambda e: e.tensor_scalar(strow2[:, 1:2], strow2[:, 1:2], 1.0 / D, None, ALU.mult))
            s_wait("vs2", m2d)
            smu = s_op(lambda e: e.activation(strow2[:, 2:3], strow2[:, 0:1], AF.Square))
            v_wait("ss2", smu)
            vd = v_op(lambda e: e.tensor_tensor(strow2[:, 1:2], strow2[:, 1:2], strow2[:, 2:3],
                                                ALU.subtract))
            s_wait("vs2", vd)
            s_op(lambda e: e.activation(strow2[:, 1:2], strow2[:, 1:2], AF.Ln, bias=epsc[:, :]))
            rsd = s_op(lambda e: e.activation(strow2[:, 1:2], strow2[:, 1:2], AF.Exp, scale=-0.5))
            v_wait("ss2", rsd)
            p.W("vector", "out", out_cnt)
            v_op(lambda e, sl=sl: e.tensor_scalar(orow[:, :], yrow[sl][:, :], strow2[:, 0:1],
                                                  None, ALU.subtract), inc=False)
            od = v_op(lambda e: e.tensor_scalar(orow[:, :], orow[:, :], strow2[:, 1:2],
                                                None, ALU.mult))
            p.W("sync", "vs2", od)
            out_cnt = sy_dma(outR[tile * 128:(tile + 1) * 128, :], orow[:, :], sem="out")
        c4stack.close()

    for s_ in ("lsy", "cs0", "cs1", "spv", "spk", "spq", "hb0", "hb1", "cw", "cwg",
               "xr0", "xr1", "h1f0", "h1f1", "out", "sph"):
        p.W("sync", s_, p.cnt[s_])
    for s_ in ("wld", "xc0", "xc1", "h1b", "ws0", "ws1", "cwg"):
        p.W("gpsimd", s_, p.cnt[s_])
    for s_ in ("vs", "vs2"):
        p.W("vector", s_, p.cnt[s_])
    for s_ in ("ts", "ts2"):
        p.W("tensor", s_, p.cnt[s_])
    for s_ in ("ss", "ss2"):
        p.W("scalar", s_, p.cnt[s_])

    blk = stack.enter_context(nc.Block())
    p.emit(blk)
    stack.close()
    return nc



def kernel(**inputs):
    nc = build_program()
    in_maps = [stage_inputs(c, **inputs) for c in range(8)]
    res = run_bass_kernel_spmd(nc, in_maps, core_ids=list(range(8)))
    out = np.zeros((4, N, D), np.float32)
    for c in range(8):
        b, hg = c // 2, c % 2
        rows = _own_rows(hg)
        out[b, rows, :] = np.asarray(res.results[c]["outR"])
    return out
